# version 06
# baseline (speedup 1.0000x reference)
import numpy as np
import concourse.bass as bass
import concourse.mybir as mybir
from concourse.bass_utils import run_bass_kernel_spmd

F32 = mybir.dt.float32
BF16 = mybir.dt.bfloat16
ALU = mybir.AluOpType
ACTF = mybir.ActivationFunctionType
AX = mybir.AxisListType
f32 = np.float32

D = 1024
NCH = 8
H = 16
DH = 64
NF = 256
E = 8
FF = 2048
FC = 16
SEQ = 2048
T = 4096
TT = 256
NTS = SEQ // TT
NT = T // TT
NSUB = TT // 128

C1 = 64.0 ** -0.25
NEGH = -(C1 * C1) / 2.0
LNRATIO = float(np.log(1.0 / np.sqrt(NF)))
EPSR = 1e-6 / 16.0
LN_EPS = 1e-5

ENGINES = ("sync", "tensor", "vector", "scalar", "gpsimd")


class Prog:
    def __init__(self):
        self.items = {e: [] for e in ENGINES}
        self.tot = {}
        self.self_contrib = {e: {} for e in ENGINES}
        self.last_wait = {e: {} for e in ENGINES}

    def sem(self, name):
        self.tot.setdefault(name, 0)

    def op(self, eng, fn, waits=(), incs=(), force=False):
        w = []
        flat = []
        for tok in waits:
            if tok is None:
                continue
            if isinstance(tok, list):
                flat.extend(t for t in tok if t is not None)
            else:
                flat.append(tok)
        for tok in flat:
            s, v = tok
            assert v <= self.tot[s], (s, v, self.tot[s])
            if v <= self.last_wait[eng].get(s, 0):
                continue
            if not force and v <= self.self_contrib[eng].get(s, 0):
                continue
            self.last_wait[eng][s] = v
            w.append((s, v))
        out = None
        for s, d in incs:
            self.tot[s] += d
            self.self_contrib[eng][s] = self.tot[s]
            out = (s, self.tot[s])
        self.items[eng].append((fn, tuple(w), tuple(incs)))
        return out


def build_program(debug=False):
    nc = bass.Bass("TRN2", target_bir_lowering=False, debug=False,
                   detect_race_conditions=False, dynamic_dma_scratch_size=2048)
    p = Prog()

    xh = nc.dram_tensor("x", [D, T], F32, kind="ExternalInput").ap()
    wq = nc.dram_tensor("wq", [D, D], F32, kind="ExternalInput").ap()
    wk = nc.dram_tensor("wk", [D, D], F32, kind="ExternalInput").ap()
    wv = nc.dram_tensor("wv", [D, D], F32, kind="ExternalInput").ap()
    wo = nc.dram_tensor("wo", [D, D], F32, kind="ExternalInput").ap()
    bqh = nc.dram_tensor("bq", [1, D], F32, kind="ExternalInput").ap()
    bkh = nc.dram_tensor("bk", [1, D], F32, kind="ExternalInput").ap()
    bvh = nc.dram_tensor("bv", [1, D], F32, kind="ExternalInput").ap()
    boh = nc.dram_tensor("bo", [1, D], F32, kind="ExternalInput").ap()
    projh = nc.dram_tensor("proj", [DH, NF], F32, kind="ExternalInput").ap()
    wgh = nc.dram_tensor("wg", [D, E], F32, kind="ExternalInput").ap()
    bgh = nc.dram_tensor("bg", [1, E], F32, kind="ExternalInput").ap()
    w1h = nc.dram_tensor("w1", [E, D, FF], F32, kind="ExternalInput").ap()
    b1h = nc.dram_tensor("b1", [E, FF], F32, kind="ExternalInput").ap()
    w2h = nc.dram_tensor("w2", [E, FF, D], F32, kind="ExternalInput").ap()
    b2h = nc.dram_tensor("b2", [E, D], F32, kind="ExternalInput").ap()
    gamh = nc.dram_tensor("gam", [1, D], F32, kind="ExternalInput").ap()
    beth = nc.dram_tensor("bet", [1, D], F32, kind="ExternalInput").ap()
    eyeh = nc.dram_tensor("eye", [128, 128], F32, kind="ExternalInput").ap()
    sel8h = nc.dram_tensor("sel8", [E, E * 128], F32, kind="ExternalInput").ap()
    outh = nc.dram_tensor("out", [D, T], F32, kind="ExternalOutput").ap()
    w1s = nc.dram_tensor("w1scr", [E, FC, 128, 2, NCH, 128], BF16,
                         kind="Internal").ap()
    w2s = nc.dram_tensor("w2scr", [E, FC, 128, 2, D], BF16,
                         kind="Internal").ap()
    dbg = {}
    if debug:
        dbg["y1"] = nc.dram_tensor("dbg_y1", [D, T], F32,
                                   kind="ExternalOutput").ap()
        dbg["mix"] = nc.dram_tensor("dbg_mix", [D, T], F32,
                                    kind="ExternalOutput").ap()

    def sb(name, shape, dtype=F32):
        return nc.alloc_sbuf_tensor(name, shape, dtype).ap()

    REG = sb("reg", [128, NCH, T])
    BVS = REG[0:1, 0, 3072:4096].rearrange("p (o2 d) -> p o2 d", o2=1)
    KV = sb("kvsb", [128, 2, H, DH + 1])
    WST = sb("wst", [128, 2, NCH, 128])
    CPD = sb("cpd", [128, NF])
    NEG1 = sb("neg1", [128, NF])
    ONES = sb("ones", [128, 128])
    EYE = sb("eye_s", [128, 128])
    SEL8 = sb("sel8_s", [E, E * 128])
    B2S = sb("b2s", [E, D])
    BQ = sb("bq_s", [128, NCH])
    BK = sb("bk_s", [128, NCH])
    BO = sb("bo_s", [128, NCH])
    GAM = sb("gam_s", [128, NCH])
    BET = sb("bet_s", [128, NCH])
    B1S = sb("b1s", [128, E, FC])
    WGS = sb("wgs", [128, NCH, E])
    WGHB = sb("wghb", [128, NCH, E], BF16)
    WGLB = sb("wglb", [128, NCH, E], BF16)
    BGS = sb("bgs", [1, E])
    WTM = sb("wtm", [128, NSUB, E])
    WFM8 = sb("wfm8", [E, TT])
    GS = sb("gs", [128, 40])

    SCRB_W = 18432
    SCRB = sb("scrb", [128, SCRB_W], BF16)
    AR_W = nc.sbuf_bytes_remaining // 4 - 16
    SCR = sb("scr", [128, AR_W])

    def view_t(base_ap, width, off, shape):
        n = int(np.prod(shape))
        assert off + n <= width, (off, shape, width)
        a = base_ap[:, off:off + n]
        if len(shape) == 1:
            return a
        if len(shape) == 2:
            return a.rearrange("p (a b) -> p a b", a=shape[0])
        if len(shape) == 3:
            return a.rearrange("p (a b c) -> p a b c", a=shape[0], b=shape[1])
        return a.rearrange("p (a b c d) -> p a b c d", a=shape[0],
                           b=shape[1], c=shape[2])

    def view(off, shape):
        return view_t(SCR, AR_W, off, shape)

    def viewb(off, shape):
        return view_t(SCRB, SCRB_W, off, shape)

    o = [0]

    def take(n):
        r = o[0]
        o[0] += n
        return r

    o[0] = 0
    P1_XT = view(take(NCH * TT), (NCH, TT))
    _b = take(2 * NCH * 128)
    P1_WMV = view(_b, (2, NCH, 128))
    P1_KQ = view(_b, (4, TT))
    P1_VAUG = view(take(NSUB * 8 * (DH + 1)), (NSUB, 8, DH + 1))
    P1_SQ2 = view(take(TT), (TT,))
    P1_KP = view(take(NSUB * NF), (NSUB, NF))
    p1_end = o[0]
    o[0] = 0
    P2_XT = view(take(NCH * TT), (NCH, TT))
    P2_KQ = view(take(4 * TT), (4, TT))
    P2_ZIN = view(take(NCH * TT), (NCH, TT))
    P2_SQ2 = view(take(TT), (TT,))
    P2_QP = view(take(2 * TT), (2, TT))
    P2_STRIP = view(take(2 * TT), (2, TT))
    P2_SQS = P2_ZIN
    p2_end = o[0]
    o[0] = 0
    F_MIX = view(take(NCH * TT), (NCH, TT))
    F_SQS = view(take(NCH * TT), (NCH, TT))
    F_WBC = view(take(TT), (TT,))
    F_RELU = view(take(TT), (TT,))
    F_STRIP = view(take(2 * TT), (2, TT))
    f_end = o[0]
    assert max(p1_end, p2_end, f_end) <= AR_W, (p1_end, p2_end, f_end, AR_W)
    o[0] = 0
    YHB = viewb(take(NCH * TT), (NCH, TT))
    YLB = viewb(take(NCH * TT), (NCH, TT))
    HTH = viewb(take(FC * TT), (FC, TT))
    HTL = viewb(take(FC * TT), (FC, TT))
    WSB = viewb(take(2 * 2 * NCH * 128), (2, 2, NCH, 128))
    W2B = viewb(take(2 * 2 * 512), (2, 2, 512))
    fb_end = o[0]
    assert fb_end <= SCRB_W, (fb_end, SCRB_W)
    PPB = viewb(0, (2, 2, NCH, 128))

    PS = nc.alloc_psum_tensor("ps", [128, 8, 512], F32).ap()

    for s in ("sC", "sX", "sW0", "sW1", "sM0", "sM1", "sPE", "sDV", "sAC",
              "sOut", "sPP"):
        p.sem(s)

    guard = []

    def dma(out_ap, in_ap, sem, waits=(), slow=False):
        return p.op("sync",
                    lambda e, o_=out_ap, i_=in_ap, s_=slow: e.dma_start(
                        out=o_, in_=i_, allow_slow_non_contiguous=s_),
                    waits=list(waits) + guard, incs=((sem, 16),))

    def mm(out_ap, lhsT, rhs, start, stop, waits=(), inc=False):
        return p.op("tensor",
                    lambda e, o_=out_ap, l_=lhsT, r_=rhs, s0=start, s1=stop:
                    e.matmul(o_, l_, r_, start=s0, stop=s1),
                    waits=list(waits) + guard,
                    incs=((("sPE", 1),) if inc else ()))

    def vop(fn, waits=(), force=False):
        return p.op("vector", fn, waits=list(waits) + guard,
                    incs=(("sDV", 1),), force=force)

    def aop(fn, waits=(), force=False):
        return p.op("scalar", fn, waits=list(waits) + guard,
                    incs=(("sAC", 1),), force=force)

    def set_guard():
        guard.clear()
        guard.extend([("sPE", p.tot["sPE"]), ("sDV", p.tot["sDV"]),
                      ("sAC", p.tot["sAC"]), ("sOut", p.tot["sOut"])])

    ncw = [0]

    def cdma(out_ap, in_ap, slow=False):
        ncw[0] += 1
        return dma(out_ap, in_ap, "sC", slow=slow)

    cdma(CPD[0:DH, :], projh[:, :])
    cdma(CPD[DH:128, :], projh[:, :])
    cdma(EYE[:, :], eyeh[:, :])
    cdma(SEL8[:, :], sel8h[:, :])
    cdma(B2S[:, :], b2h[:, :])
    cdma(BQ[:, :], bqh.rearrange("o (c p) -> (o p) c", p=128), slow=True)
    cdma(BK[:, :], bkh.rearrange("o (c p) -> (o p) c", p=128), slow=True)
    cdma(BO[:, :], boh.rearrange("o (c p) -> (o p) c", p=128), slow=True)
    cdma(GAM[:, :], gamh.rearrange("o (c p) -> (o p) c", p=128), slow=True)
    cdma(BET[:, :], beth.rearrange("o (c p) -> (o p) c", p=128), slow=True)
    cdma(B1S[:, :, :], b1h.rearrange("e (fc p) -> p e fc", p=128), slow=True)
    cdma(WGS[:, :, :], wgh.rearrange("(c p) e -> p c e", p=128))
    cdma(BGS[:, :], bgh[:, :])
    cdma(BVS[0:1, 0, :], bvh[:, :])
    call = ("sC", 16 * ncw[0])
    t_scale = vop(lambda e: e.tensor_scalar_mul(CPD[:, :], CPD[:, :],
                                                float(C1)), waits=[call])
    vop(lambda e: e.memset(NEG1[:, :], float(NEGH)))
    vop(lambda e: e.memset(ONES[:, :], 1.0))
    twg = vop(lambda e: e.tensor_copy(out=WGHB[:, :, :], in_=WGS[:, :, :]),
              waits=[call])
    twg = vop(lambda e: e.tensor_sub(WGLB[:, :, :], WGS[:, :, :],
                                     WGHB[:, :, :]), waits=[twg], force=True)
    cready = [call, t_scale, twg]

    wst_free = [None, None]
    wst_ready = [None, None]
    wst_i = [0]

    def wst_load(in_ap, ap_dst=None):
        slot = wst_i[0] % 2
        wst_i[0] += 1
        dst = WST[:, slot, :, :] if ap_dst is None else ap_dst(slot)
        tok = dma(dst, in_ap, "sW0" if slot == 0 else "sW1",
                  waits=[wst_free[slot]])
        wst_ready[slot] = tok
        return slot

    mv_free = [None, None]
    mv_ready = [None, None]
    mv_i = [0]

    def mv_load(dst_ap, in_ap, extra=()):
        slot = mv_i[0] % 2
        mv_i[0] += 1
        tok = dma(dst_ap(slot), in_ap, "sM0" if slot == 0 else "sM1",
                  waits=[mv_free[slot]] + list(extra))
        mv_ready[slot] = tok
        return slot

    xt_state = {"free": None}

    def x_load(buf, t0):
        tok = dma(buf[:, :, :],
                  xh.rearrange("(c p) t -> p c t", p=128)[:, :, t0:t0 + TT],
                  "sX", waits=[xt_state["free"]])
        return tok

    bank_free = [None] * 8
    strip_free = [None]

    wkslice = wk.rearrange("(c p) d -> p c d", p=128)
    wqslice = wq.rearrange("(c p) d -> p c d", p=128)
    wvslice = wv.rearrange("(c p) d -> p c d", p=128)
    woslice = wo.rearrange("(c p) d -> p c d", p=128)
    w1slice = w1h.rearrange("e (c p) f -> p e c f", p=128)
    w2slice = w2h.rearrange("e (f p) d -> p e f d", p=128)

    PPF = [REG[:, 6, 3072:4096].rearrange("p (a b) -> p a b", a=NCH),
           REG[:, 7, 3072:4096].rearrange("p (a b) -> p a b", a=NCH)]
    pp_state = {"i": 0, "free": [None, None], "bfree": [None, None]}
    pp_slices = [("w1", e_i, fcg) for e_i in range(E) for fcg in range(FC)]
    pp_slices += [("w2", e_i, fcg) for e_i in range(E) for fcg in range(FC)]

    def emit_prepass(nslices):
        for _ in range(nslices):
            i = pp_state["i"]
            if i >= len(pp_slices):
                return
            pp_state["i"] = i + 1
            kind, e_i, fcg = pp_slices[i]
            slot = i % 2
            if kind == "w1":
                src = w1slice[:, e_i, :, fcg * 128:(fcg + 1) * 128]
            else:
                src = w2slice[:, e_i, fcg, :].rearrange(
                    "p (a b) -> p a b", a=NCH)
            tin = dma(PPF[slot][:, :, :], src, "sPP",
                      waits=[pp_state["free"][slot]])
            th = vop(lambda e, s_=slot: e.tensor_copy(
                out=PPB[:, s_, 0, :, :], in_=PPF[s_][:, :, :]),
                waits=[tin, pp_state["bfree"][slot]])
            tl = vop(lambda e, s_=slot: e.tensor_sub(
                PPB[:, s_, 1, :, :], PPF[s_][:, :, :], PPB[:, s_, 0, :, :]),
                waits=[th], force=True)
            pp_state["free"][slot] = tl
            if kind == "w1":
                dsth = w1s[e_i, fcg, :, 0, :, :]
                dstl = w1s[e_i, fcg, :, 1, :, :]
            else:
                dsth = w2s[e_i, fcg, :, 0, :].rearrange(
                    "p (a b) -> p a b", a=NCH)
                dstl = w2s[e_i, fcg, :, 1, :].rearrange(
                    "p (a b) -> p a b", a=NCH)
            dma(dsth, PPB[:, slot, 0, :, :], "sPP", waits=[tl])
            tout = dma(dstl, PPB[:, slot, 1, :, :], "sPP", waits=[tl])
            pp_state["bfree"][slot] = tout

    def layernorm(buf, sqs, strip, src_toks, affine, extra_sq_waits=()):
        tsq = aop(lambda e, b_=buf, q_=sqs: e.activation(
            out=q_[:, :, :], in_=b_[:, :, :], func=ACTF.Square, bias=f32(0.0),
            scale=f32(1.0)), waits=list(src_toks) + list(extra_sq_waits))
        t6 = None
        for c in range(NCH):
            t6 = mm(PS[0:1, 6, 0:TT], ONES[:, 0:1], buf[:, c, :], c == 0,
                    c == NCH - 1, waits=[bank_free[6]] + list(src_toks),
                    inc=(c == NCH - 1))
        t7 = None
        for c in range(NCH):
            t7 = mm(PS[0:1, 7, 0:TT], ONES[:, 0:1], sqs[:, c, :], c == 0,
                    c == NCH - 1, waits=[bank_free[7], tsq],
                    inc=(c == NCH - 1))
        tm = vop(lambda e, s_=strip: e.tensor_scalar_mul(
            s_[0:1, 0, :], PS[0:1, 6, 0:TT], float(1.0 / D)),
            waits=[t6, strip_free[0]])
        bank_free[6] = tm
        tmm = vop(lambda e, s_=strip: e.tensor_mul(
            s_[0:1, 1, :], s_[0:1, 0, :], s_[0:1, 0, :]),
            waits=[tm], force=True)
        tv = vop(lambda e, s_=strip: e.scalar_tensor_tensor(
            out=s_[0:1, 1, :], in0=PS[0:1, 7, 0:TT], scalar=float(1.0 / D),
            in1=s_[0:1, 1, :], op0=ALU.mult, op1=ALU.subtract),
            waits=[t7, tmm], force=True)
        bank_free[7] = tv
        ts = aop(lambda e, s_=strip: e.activation(
            out=s_[0:1, 1, :], in_=s_[0:1, 1, :], func=ACTF.Sqrt,
            bias=f32(LN_EPS), scale=f32(1.0)), waits=[tv])
        tr = vop(lambda e, s_=strip: e.reciprocal(s_[0:1, 1, :],
                                                  s_[0:1, 1, :]),
                 waits=[ts])
        tb4 = mm(PS[:, 4, 0:TT], ONES[0:1, 0:128], strip[0:1, 0, :], True,
                 True, waits=[bank_free[4], tm], inc=True)
        tb5 = mm(PS[:, 5, 0:TT], ONES[0:1, 0:128], strip[0:1, 1, :], True,
                 True, waits=[bank_free[5], tr], inc=True)
        strip_free[0] = tb5
        last = None
        for c in range(NCH):
            vop(lambda e, c_=c, b_=buf: e.tensor_sub(
                b_[:, c_, :], b_[:, c_, :], PS[:, 4, 0:TT]), waits=[tb4])
            last = vop(lambda e, c_=c, b_=buf: e.tensor_mul(
                b_[:, c_, :], b_[:, c_, :], PS[:, 5, 0:TT]), waits=[tb5])
            if affine:
                last = vop(lambda e, c_=c, b_=buf: e.tensor_scalar(
                    out=b_[:, c_, :], in0=b_[:, c_, :],
                    scalar1=GAM[:, c_:c_ + 1], scalar2=BET[:, c_:c_ + 1],
                    op0=ALU.mult, op1=ALU.add))
        bank_free[4] = last
        bank_free[5] = last
        return last

    vaug_free = None
    breg_free = []
    for s in range(2):
        set_guard()
        kvz = vop(lambda e: e.memset(KV[:, :, :, :], 0.0),
                  waits=(cready if s == 0 else ()))
        vones = vop(lambda e: e.memset(P1_VAUG[:, :, :, DH:DH + 1], 1.0))
        for ti in range(NTS):
            t0 = s * SEQ + ti * TT
            if s == 0:
                emit_prepass(16)
            xw = x_load(P1_XT, t0)
            kv_last_mm = None
            for half in range(2):
                vlast = None
                for oc in range(4):
                    oct_ = half * 4 + oc
                    ms = mv_load(lambda sl: P1_WMV[:, sl, :, :],
                                 wvslice[:, :, oct_ * 128:(oct_ + 1) * 128],
                                 extra=breg_free if oc == 0 else ())
                    for tt in range(NSUB):
                        b = (oc * NSUB + tt) % 2
                        mm(PS[:, b, 0:128], ONES[0:1, 0:128],
                           BVS[0:1, 0, oct_ * 128:(oct_ + 1) * 128], True,
                           False,
                           waits=[bank_free[b], mv_ready[ms], xw] + cready)
                        tok = None
                        for c in range(NCH):
                            tok = mm(PS[:, b, 0:128],
                                     P1_XT[:, c, tt * 128:(tt + 1) * 128],
                                     P1_WMV[:, ms, c, :], False, c == NCH - 1,
                                     inc=(c == NCH - 1))
                        if tt == NSUB - 1:
                            mv_free[ms] = tok
                        vlast = vop(lambda e, b_=b, tt_=tt, q_=oc:
                                    e.tensor_copy(
                                        out=P1_VAUG[:, tt_,
                                                    q_ * 2:q_ * 2 + 2, 0:DH],
                                        in_=PS[:, b_, 0:128].rearrange(
                                            "p (a b) -> p a b", a=2)),
                                    waits=[tok, vaug_free])
                        bank_free[b] = vlast
                vdone = vlast
                evtoks = []
                for cl in range(4):
                    cglob = half * 4 + cl
                    ws = wst_load(wkslice[:, :, cglob * 128:(cglob + 1) * 128])
                    b = 2 + (cl % 2)
                    tok = None
                    for c in range(NCH):
                        tok = mm(PS[:, b, 0:TT], WST[:, ws, c, :],
                                 P1_XT[:, c, :], c == 0, c == NCH - 1,
                                 waits=([bank_free[b], wst_ready[ws], xw,
                                         mv_free[0], mv_free[1]]
                                        if c == 0 else ()),
                                 inc=(c == NCH - 1))
                    wst_free[ws] = tok
                    ev = aop(lambda e, b_=b, cl_=cl, cg=cglob:
                             e.activation(out=P1_KQ[:, cl_, :],
                                          in_=PS[:, b_, 0:TT],
                                          func=ACTF.Identity,
                                          bias=BK[:, cg:cg + 1],
                                          scale=f32(1.0)),
                             waits=[tok])
                    evtoks.append(ev)
                    bank_free[b] = ev
                sq2_free = None
                tsq = None
                for cl in range(4):
                    cglob = half * 4 + cl
                    tsq = aop(lambda e, cl_=cl:
                              e.activation(out=P1_SQ2[:, :],
                                           in_=P1_KQ[:, cl_, :],
                                           func=ACTF.Square, bias=f32(0.0),
                                           scale=f32(1.0)),
                              waits=[evtoks[cl], sq2_free])
                    for dlt in range(2):
                        h = 2 * cglob + dlt
                        hloc = h - 8 * half
                        base = dlt * 64
                        eps_toks = []
                        for tt in range(NSUB):
                            b = tt % 2
                            mm(PS[:, b, 0:NF],
                               P1_KQ[base:base + DH, cl,
                                     tt * 128:(tt + 1) * 128],
                               CPD[base:base + DH, :], True, False,
                               waits=[bank_free[b], evtoks[cl]])
                            tok = mm(PS[:, b, 0:NF],
                                     P1_SQ2[base:base + DH,
                                            tt * 128:(tt + 1) * 128],
                                     NEG1[base:base + DH, 0:NF], False, True,
                                     waits=[tsq], inc=True)
                            sq2_free = tok
                            te = aop(lambda e, b_=b, tt_=tt:
                                     e.activation(out=P1_KP[:, tt_, :],
                                                  in_=PS[:, b_, 0:NF],
                                                  func=ACTF.Exp,
                                                  bias=f32(LNRATIO),
                                                  scale=f32(1.0)),
                                     waits=[tok])
                            bank_free[b] = te
                            tp = vop(lambda e, tt_=tt:
                                     e.tensor_scalar_add(P1_KP[:, tt_, :],
                                                         P1_KP[:, tt_, :],
                                                         float(EPSR)),
                                     waits=[te])
                            eps_toks.append(tp)
                        for fc2 in range(2):
                            b = 4 + fc2
                            tok = None
                            for tt in range(NSUB):
                                tok = mm(PS[:, b, 0:DH + 1],
                                         P1_KP[:, tt,
                                               fc2 * 128:(fc2 + 1) * 128],
                                         P1_VAUG[:, tt, hloc, :], tt == 0,
                                         tt == NSUB - 1,
                                         waits=([bank_free[b], eps_toks[tt],
                                                 vdone, kvz, vones]
                                                if tt == 0 else
                                                [eps_toks[tt]]),
                                         inc=(tt == NSUB - 1))
                            kv_last_mm = tok
                            tkv = vop(lambda e, b_=b, f_=fc2, h_=h:
                                      e.tensor_add(KV[:, f_, h_, :],
                                                   KV[:, f_, h_, :],
                                                   PS[:, b_, 0:DH + 1]),
                                      waits=[tok])
                            bank_free[b] = tkv
                vaug_free = kv_last_mm
                breg_free = [kv_last_mm, tsq]
            xt_state["free"] = kv_last_mm
        kv_done_dv = ("sDV", p.tot["sDV"])
        set_guard()
        for ti in range(NTS):
            t0 = s * SEQ + ti * TT
            if s == 0:
                emit_prepass(16)
            xw = x_load(P2_XT, t0)
            zin_last = None
            sq2_free = None
            kq_free = None
            for half in range(2):
                evtoks = []
                for cl in range(4):
                    cglob = half * 4 + cl
                    ws = wst_load(wqslice[:, :, cglob * 128:(cglob + 1) * 128])
                    b = 2 + (cl % 2)
                    tok = None
                    for c in range(NCH):
                        tok = mm(PS[:, b, 0:TT], WST[:, ws, c, :],
                                 P2_XT[:, c, :], c == 0, c == NCH - 1,
                                 waits=([bank_free[b], wst_ready[ws], xw]
                                        if c == 0 else ()),
                                 inc=(c == NCH - 1))
                    wst_free[ws] = tok
                    ev = aop(lambda e, b_=b, cl_=cl, cg=cglob:
                             e.activation(out=P2_KQ[:, cl_, :],
                                          in_=PS[:, b_, 0:TT],
                                          func=ACTF.Identity,
                                          bias=BQ[:, cg:cg + 1],
                                          scale=f32(1.0)),
                             waits=[tok, kq_free])
                    evtoks.append(ev)
                    bank_free[b] = ev
                for cl in range(4):
                    cglob = half * 4 + cl
                    tsq = aop(lambda e, cl_=cl:
                              e.activation(out=P2_SQ2[:, :],
                                           in_=P2_KQ[:, cl_, :],
                                           func=ACTF.Square, bias=f32(0.0),
                                           scale=f32(1.0)),
                              waits=[evtoks[cl], sq2_free])
                    for dlt in range(2):
                        h = 2 * cglob + dlt
                        base = dlt * 64
                        eps_toks = []
                        for fc2 in range(2):
                            b = fc2
                            mm(PS[:, b, 0:TT],
                               CPD[base:base + DH, fc2 * 128:(fc2 + 1) * 128],
                               P2_KQ[base:base + DH, cl, :], True, False,
                               waits=[bank_free[b], evtoks[cl]])
                            tok = mm(PS[:, b, 0:TT],
                                     NEG1[base:base + DH, 0:128],
                                     P2_SQ2[base:base + DH, :], False, True,
                                     waits=[tsq], inc=True)
                            sq2_free = tok
                            te = aop(lambda e, b_=b, f_=fc2:
                                     e.activation(out=P2_QP[:, f_, :],
                                                  in_=PS[:, b_, 0:TT],
                                                  func=ACTF.Exp,
                                                  bias=f32(LNRATIO),
                                                  scale=f32(1.0)),
                                     waits=[tok])
                            bank_free[b] = te
                            tp = vop(lambda e, f_=fc2:
                                     e.tensor_scalar_add(P2_QP[:, f_, :],
                                                         P2_QP[:, f_, :],
                                                         float(EPSR)),
                                     waits=[te])
                            eps_toks.append(tp)
                        ba = 4 + (h % 2)
                        bn = 6 + (h % 2)
                        tok = None
                        for fc2 in range(2):
                            tok = mm(PS[base:base + DH, ba, 0:TT],
                                     KV[:, fc2, h, 0:DH],
                                     P2_QP[:, fc2, :], fc2 == 0, fc2 == 1,
                                     waits=([bank_free[ba], eps_toks[0],
                                             eps_toks[1], kv_done_dv]
                                            if fc2 == 0 else ()),
                                     inc=(fc2 == 1))
                        qkv_mm = tok
                        tok = None
                        for fc2 in range(2):
                            tok = mm(PS[0:1, bn, 0:TT],
                                     KV[:, fc2, h, DH:DH + 1],
                                     P2_QP[:, fc2, :], fc2 == 0, fc2 == 1,
                                     waits=([bank_free[bn]] if fc2 == 0
                                            else ()),
                                     inc=(fc2 == 1))
                        nrm_mm = tok
                        tn = vop(lambda e, b_=bn:
                                 e.tensor_scalar_add(P2_STRIP[0:1, 0, :],
                                                     PS[0:1, b_, 0:TT],
                                                     float(1e-6)),
                                 waits=[nrm_mm, strip_free[0]])
                        tn = vop(lambda e: e.reciprocal(P2_STRIP[0:1, 1, :],
                                                        P2_STRIP[0:1, 0, :]),
                                 waits=[tn], force=True)
                        tb = mm(PS[:, bn, 0:TT], ONES[0:1, 0:128],
                                P2_STRIP[0:1, 1, :], True, True,
                                waits=[tn], inc=True)
                        strip_free[0] = tb
                        vop(lambda e, b_=ba, bs=base, c2=cglob:
                            e.tensor_copy(out=P2_ZIN[bs:bs + DH, c2, :],
                                          in_=PS[bs:bs + DH, b_, 0:TT]),
                            waits=[qkv_mm])
                        zin_last = vop(
                            lambda e, bn_=bn, bs=base, c2=cglob:
                            e.tensor_mul(P2_ZIN[bs:bs + DH, c2, :],
                                         P2_ZIN[bs:bs + DH, c2, :],
                                         PS[bs:bs + DH, bn_, 0:TT]),
                            waits=[tb])
                        bank_free[ba] = zin_last
                        bank_free[bn] = zin_last
                    kq_free = ("sPE", p.tot["sPE"])
            ev_toks = []
            wo_mm_last = None
            for dc in range(NCH):
                ws = wst_load(woslice[:, :, dc * 128:(dc + 1) * 128])
                b = 2 + (dc % 2)
                tok = None
                for c in range(NCH):
                    tok = mm(PS[:, b, 0:TT], WST[:, ws, c, :],
                             P2_ZIN[:, c, :], c == 0, c == NCH - 1,
                             waits=([bank_free[b], wst_ready[ws], zin_last]
                                    if c == 0 else ()),
                             inc=(c == NCH - 1))
                wst_free[ws] = tok
                wo_mm_last = tok
                ev = vop(lambda e, b_=b, dc_=dc:
                         e.scalar_tensor_tensor(
                             out=P2_XT[:, dc_, :], in0=PS[:, b_, 0:TT],
                             scalar=BO[:, dc_:dc_ + 1],
                             in1=P2_XT[:, dc_, :],
                             op0=ALU.add, op1=ALU.add),
                         waits=[tok])
                ev_toks.append(ev)
                bank_free[b] = ev
            layernorm(P2_XT, P2_SQS, P2_STRIP, ev_toks, affine=False,
                      extra_sq_waits=[wo_mm_last])
            cplast = None
            for c in range(NCH):
                cplast = vop(lambda e, c_=c, t0_=t0: e.tensor_copy(
                    out=REG[:, c_, t0_:t0_ + TT], in_=P2_XT[:, c_, :]))
            xt_state["free"] = cplast
            if debug:
                dma(dbg["y1"].rearrange("(c p) t -> p c t", p=128)
                    [:, :, t0:t0 + TT], REG[:, :, t0:t0 + TT], "sOut",
                    waits=[cplast])

    ppdone = ("sPP", p.tot["sPP"])
    mix_free = None
    ht_free = None
    yb_free = None
    for ti in range(NT):
        set_guard()
        if ti == 0:
            guard.append(ppdone)
        t0 = ti * TT
        regslot = REG[:, :, t0:t0 + TT]
        ylast = None
        for c in range(NCH):
            th = vop(lambda e, c_=c, r_=regslot: e.tensor_copy(
                out=YHB[:, c_, :], in_=r_[:, c_, :]), waits=[yb_free])
            ylast = vop(lambda e, c_=c, r_=regslot: e.tensor_sub(
                YLB[:, c_, :], r_[:, c_, :], YHB[:, c_, :]),
                waits=[th], force=True)
        wtoks = []
        for tt in range(NSUB):
            b = 6 + (tt % 2)
            mm(PS[:, b, 0:E], ONES[0:1, 0:128], BGS[0:1, :], True, False,
               waits=[bank_free[b], ylast])
            tok = None
            for c in range(NCH):
                sl = slice(tt * 128, (tt + 1) * 128)
                mm(PS[:, b, 0:E], YHB[:, c, sl], WGHB[:, c, :], False, False)
                mm(PS[:, b, 0:E], YHB[:, c, sl], WGLB[:, c, :], False, False)
                tok = mm(PS[:, b, 0:E], YLB[:, c, sl], WGHB[:, c, :], False,
                         c == NCH - 1, inc=(c == NCH - 1))
            t1 = vop(lambda e, b_=b: e.reduce_max(GS[:, 24:25],
                                                  PS[:, b_, 0:E], axis=AX.X),
                     waits=[tok])
            teq = vop(lambda e, b_=b: e.tensor_scalar(
                out=GS[:, 0:8], in0=PS[:, b_, 0:E], scalar1=GS[:, 24:25],
                scalar2=None, op0=ALU.is_equal), waits=[t1], force=True)
            tst = vop(lambda e, b_=b: e.scalar_tensor_tensor(
                out=GS[:, 0:8], in0=GS[:, 0:8], scalar=float(-1e30),
                in1=PS[:, b_, 0:E], op0=ALU.mult, op1=ALU.add),
                waits=[teq], force=True)
            tm2 = vop(lambda e: e.reduce_max(GS[:, 26:27], GS[:, 0:8],
                                             axis=AX.X), waits=[tst],
                      force=True)
            tneg = vop(lambda e: e.tensor_scalar_mul(GS[:, 25:26],
                                                     GS[:, 24:25],
                                                     float(-1.0)))
            tgem = vop(lambda e, b_=b: e.tensor_scalar(
                out=GS[:, 8:16], in0=PS[:, b_, 0:E], scalar1=GS[:, 26:27],
                scalar2=None, op0=ALU.is_ge), waits=[tm2], force=True)
            tel = aop(lambda e, b_=b: e.activation(
                out=GS[:, 16:24], in_=PS[:, b_, 0:E], func=ACTF.Exp,
                bias=GS[:, 25:26], scale=f32(1.0)), waits=[tneg, tgem])
            tdn = aop(lambda e: e.activation(
                out=GS[:, 27:28], in_=GS[:, 26:27], func=ACTF.Exp,
                bias=GS[:, 25:26], scale=f32(1.0)))
            bank_free[b] = [tgem, tel]
            tda = vop(lambda e: e.tensor_scalar_add(GS[:, 27:28],
                                                    GS[:, 27:28],
                                                    float(1.0)), waits=[tdn])
            trdn = vop(lambda e: e.reciprocal(GS[:, 28:29], GS[:, 27:28]),
                       waits=[tda], force=True)
            vop(lambda e, tt_=tt: e.tensor_mul(WTM[:, tt_, :], GS[:, 16:24],
                                               GS[:, 8:16]),
                waits=[tel, tgem], force=True)
            tw = vop(lambda e, tt_=tt: e.tensor_scalar_mul(
                WTM[:, tt_, :], WTM[:, tt_, :], GS[:, 28:29]),
                waits=[trdn], force=True)
            wtoks.append(tw)
        trtoks = []
        for tt in range(NSUB):
            tok = p.op("tensor",
                       lambda e, tt_=tt: e.transpose(
                           PS[0:E, 7, tt_ * 128:(tt_ + 1) * 128],
                           WTM[:, tt_, :], EYE[:, :]),
                       waits=[wtoks[tt], bank_free[7]] + guard,
                       incs=(("sPE", 1),))
            trtoks.append(tok)
        tw8 = vop(lambda e: e.tensor_copy(out=WFM8[:, :],
                                          in_=PS[0:E, 7, 0:TT]),
                  waits=trtoks)
        bank_free[7] = tw8
        eo_ev_last = None
        for e_i in range(E):
            twb = mm(PS[:, 6, 0:TT], SEL8[:, e_i * 128:(e_i + 1) * 128],
                     WFM8[:, :], True, True,
                     waits=[tw8, bank_free[6]], inc=True)
            twbc = vop(lambda e: e.tensor_copy(out=F_WBC[:, :],
                                               in_=PS[:, 6, 0:TT]),
                       waits=[twb])
            bank_free[6] = twbc
            lim_toks = []
            for fcg in range(FC):
                ws = wst_load(w1s[e_i, fcg, :, :, :, :],
                              ap_dst=lambda sl: WSB[:, sl, :, :, :])
                b = fcg % 2
                tok = None
                for c in range(NCH):
                    mm(PS[:, b, 0:TT], WSB[:, ws, 0, c, :], YHB[:, c, :],
                       c == 0, False,
                       waits=([bank_free[b], wst_ready[ws], ylast]
                              if c == 0 else ()))
                    mm(PS[:, b, 0:TT], WSB[:, ws, 0, c, :], YLB[:, c, :],
                       False, False)
                    tok = mm(PS[:, b, 0:TT], WSB[:, ws, 1, c, :],
                             YHB[:, c, :], False, c == NCH - 1,
                             inc=(c == NCH - 1))
                wst_free[ws] = tok
                trl = aop(lambda e, b_=b, fc_=fcg, e__=e_i:
                          e.activation(out=F_RELU[:, :], in_=PS[:, b_, 0:TT],
                                       func=ACTF.Relu,
                                       bias=B1S[:, e__, fc_:fc_ + 1],
                                       scale=f32(1.0)),
                          waits=[tok, ht_free,
                                 lim_toks[-1] if lim_toks else None])
                bank_free[b] = trl
                tmu = vop(lambda e: e.tensor_mul(
                    F_RELU[:, :], F_RELU[:, :], F_WBC[:, :]),
                    waits=[trl, twbc])
                thh = vop(lambda e, fc_=fcg: e.tensor_copy(
                    out=HTH[:, fc_, :], in_=F_RELU[:, :]),
                    waits=[tmu], force=True)
                tll = vop(lambda e, fc_=fcg: e.tensor_sub(
                    HTL[:, fc_, :], F_RELU[:, :], HTH[:, fc_, :]),
                    waits=[thh], force=True)
                lim_toks.append(tll)
            for dh2 in range(2):
                gtoks = [None] * 4
                if e_i == 0:
                    for dl in range(4):
                        dc = dh2 * 4 + dl
                        gtoks[dl] = mm(PS[:, 2 + dl, 0:TT],
                                       B2S[:, dc * 128:(dc + 1) * 128],
                                       WFM8[:, :], True, False,
                                       waits=[bank_free[2 + dl], tw8])
                for fcg in range(FC):
                    wm = mv_load(lambda sl: W2B[:, sl, :, :],
                                 w2s[e_i, fcg, :, :,
                                     dh2 * 512:(dh2 + 1) * 512])
                    for dl in range(4):
                        st = (fcg == 0) and (e_i != 0)
                        sp = (fcg == FC - 1)
                        w = [mv_ready[wm], lim_toks[fcg]]
                        if fcg == 0 and e_i != 0:
                            w.append(bank_free[2 + dl])
                        sl = slice(dl * 128, (dl + 1) * 128)
                        mm(PS[:, 2 + dl, 0:TT], W2B[:, wm, 0, sl],
                           HTH[:, fcg, :], st, False, waits=w)
                        mm(PS[:, 2 + dl, 0:TT], W2B[:, wm, 0, sl],
                           HTL[:, fcg, :], False, False)
                        gtoks[dl] = mm(PS[:, 2 + dl, 0:TT],
                                       W2B[:, wm, 1, sl],
                                       HTH[:, fcg, :], False, sp,
                                       inc=(sp or dl == 3))
                    mv_free[wm] = gtoks[3]
                for dl in range(4):
                    dc = dh2 * 4 + dl
                    if e_i == 0:
                        ev = vop(lambda e, dl_=dl, dc_=dc: e.tensor_copy(
                            out=F_MIX[:, dc_, :], in_=PS[:, 2 + dl_, 0:TT]),
                            waits=[gtoks[dl], mix_free])
                    else:
                        ev = vop(lambda e, dl_=dl, dc_=dc: e.tensor_add(
                            F_MIX[:, dc_, :], F_MIX[:, dc_, :],
                            PS[:, 2 + dl_, 0:TT]),
                            waits=[gtoks[dl]])
                    bank_free[2 + dl] = ev
                    eo_ev_last = ev
            ht_free = ("sPE", p.tot["sPE"])
        yb_free = ("sPE", p.tot["sPE"])
        add_toks = []
        for c in range(NCH):
            add_toks.append(vop(lambda e, c_=c, r_=regslot: e.tensor_add(
                F_MIX[:, c_, :], F_MIX[:, c_, :], r_[:, c_, :]),
                waits=[eo_ev_last]))
        layernorm(F_MIX, F_SQS, F_STRIP, [add_toks[-1]], affine=True,
                  extra_sq_waits=[ht_free])
        add_toks = []
        for c in range(NCH):
            add_toks.append(vop(lambda e, c_=c, r_=regslot: e.tensor_add(
                F_MIX[:, c_, :], F_MIX[:, c_, :], r_[:, c_, :])))
        lnlast = layernorm(F_MIX, F_SQS, F_STRIP, [add_toks[-1]],
                           affine=False)
        if debug:
            dma(dbg["mix"].rearrange("(c p) t -> p c t", p=128)
                [:, :, t0:t0 + TT], F_MIX[:, :, :], "sOut", waits=[lnlast])
        dma(outh.rearrange("(c p) t -> p c t", p=128)[:, :, t0:t0 + TT],
            F_MIX[:, :, :], "sOut", waits=[lnlast])
        mix_free = ("sOut", p.tot["sOut"])

    p.op("sync", None, waits=[("sOut", p.tot["sOut"])])

    import contextlib
    es = contextlib.ExitStack()
    sems = {s: es.enter_context(nc.semaphore(s)) for s in p.tot}
    block = es.enter_context(nc.Block())

    def emit(eng_name):
        def body(eng):
            for fn, waits, incs in p.items[eng_name]:
                for s, v in waits:
                    eng.wait_ge(sems[s], v)
                if fn is None:
                    continue
                instr = fn(eng)
                for s, d in incs:
                    instr.then_inc(sems[s], d)
        return body

    block.sync(emit("sync"))
    block.tensor(emit("tensor"))
    block.vector(emit("vector"))
    block.scalar(emit("scalar"))
    block.gpsimd(emit("gpsimd"))
    es.close()
    return nc


def make_inmaps(inputs):
    q = np.asarray(inputs["L0_attn_queries"], np.float32)
    base = {
        "wq": np.ascontiguousarray(inputs["L0_attn_Wq"], np.float32),
        "wk": np.ascontiguousarray(inputs["L0_attn_Wk"], np.float32),
        "wv": np.ascontiguousarray(inputs["L0_attn_Wv"], np.float32),
        "wo": np.ascontiguousarray(inputs["L0_attn_Wo"], np.float32),
        "bq": np.asarray(inputs["L0_attn_bq"], np.float32).reshape(1, D),
        "bk": np.asarray(inputs["L0_attn_bk"], np.float32).reshape(1, D),
        "bv": np.asarray(inputs["L0_attn_bv"], np.float32).reshape(1, D),
        "bo": np.asarray(inputs["L0_attn_bo"], np.float32).reshape(1, D),
        "proj": np.ascontiguousarray(inputs["L0_attn_proj"], np.float32),
        "wg": np.ascontiguousarray(inputs["L0_ffn_Wg"], np.float32),
        "bg": np.asarray(inputs["L0_ffn_bg"], np.float32).reshape(1, E),
        "w1": np.ascontiguousarray(inputs["L0_ffn_W1"], np.float32),
        "b1": np.ascontiguousarray(inputs["L0_ffn_b1"], np.float32),
        "w2": np.ascontiguousarray(inputs["L0_ffn_W2"], np.float32),
        "b2": np.ascontiguousarray(inputs["L0_ffn_b2"], np.float32),
        "gam": np.asarray(inputs["L0_ffn_gamma"], np.float32).reshape(1, D),
        "bet": np.asarray(inputs["L0_ffn_beta"], np.float32).reshape(1, D),
        "eye": np.eye(128, dtype=np.float32),
        "sel8": np.kron(np.eye(E, dtype=np.float32),
                        np.ones((1, 128), np.float32)),
    }
    in_maps = []
    for c in range(8):
        m = dict(base)
        m["x"] = np.ascontiguousarray(q[2 * c:2 * c + 2].reshape(T, D).T)
        in_maps.append(m)
    return in_maps


def kernel(**inputs):
    in_maps = make_inmaps(inputs)
    nc = build_program()
    res = run_bass_kernel_spmd(nc, in_maps, core_ids=list(range(8)))
    outs = [np.asarray(r["out"]) for r in res.results]
    full = np.stack([o.T.reshape(2, SEQ, D) for o in outs])
    return np.ascontiguousarray(full.reshape(16, SEQ, D), dtype=np.float32)


# version 07
# speedup vs baseline: 1.0077x; 1.0077x over previous
import numpy as np
import concourse.bass as bass
import concourse.mybir as mybir
from concourse.bass_utils import run_bass_kernel_spmd

F32 = mybir.dt.float32
BF16 = mybir.dt.bfloat16
ALU = mybir.AluOpType
ACTF = mybir.ActivationFunctionType
AX = mybir.AxisListType
f32 = np.float32

D = 1024
NCH = 8
H = 16
DH = 64
NF = 256
E = 8
FF = 2048
FC = 16
SEQ = 2048
T = 4096
TT = 256
NTS = SEQ // TT
NT = T // TT
NSUB = TT // 128

C1 = 64.0 ** -0.25
NEGH = -(C1 * C1) / 2.0
LNRATIO = float(np.log(1.0 / np.sqrt(NF)))
EPSR = 1e-6 / 16.0
LN_EPS = 1e-5

ENGINES = ("sync", "tensor", "vector", "scalar", "gpsimd")


class Prog:
    def __init__(self):
        self.items = {e: [] for e in ENGINES}
        self.tot = {}
        self.self_contrib = {e: {} for e in ENGINES}
        self.last_wait = {e: {} for e in ENGINES}

    def sem(self, name):
        self.tot.setdefault(name, 0)

    def op(self, eng, fn, waits=(), incs=(), force=False):
        w = []
        flat = []
        for tok in waits:
            if tok is None:
                continue
            if isinstance(tok, list):
                flat.extend(t for t in tok if t is not None)
            else:
                flat.append(tok)
        for tok in flat:
            s, v = tok
            assert v <= self.tot[s], (s, v, self.tot[s])
            if v <= self.last_wait[eng].get(s, 0):
                continue
            if not force and v <= self.self_contrib[eng].get(s, 0):
                continue
            self.last_wait[eng][s] = v
            w.append((s, v))
        out = None
        for s, d in incs:
            self.tot[s] += d
            self.self_contrib[eng][s] = self.tot[s]
            out = (s, self.tot[s])
        self.items[eng].append((fn, tuple(w), tuple(incs)))
        return out


def build_program(debug=False):
    nc = bass.Bass("TRN2", target_bir_lowering=False, debug=False,
                   detect_race_conditions=False, dynamic_dma_scratch_size=2048)
    p = Prog()

    xh = nc.dram_tensor("x", [D, T], F32, kind="ExternalInput").ap()
    wq = nc.dram_tensor("wq", [NCH, 128, D], F32, kind="ExternalInput").ap()
    wk = nc.dram_tensor("wk", [NCH, 128, D], F32, kind="ExternalInput").ap()
    wv = nc.dram_tensor("wv", [NCH, 128, D], F32, kind="ExternalInput").ap()
    wo = nc.dram_tensor("wo", [NCH, 128, D], F32, kind="ExternalInput").ap()
    bqh = nc.dram_tensor("bq", [1, D], F32, kind="ExternalInput").ap()
    bkh = nc.dram_tensor("bk", [1, D], F32, kind="ExternalInput").ap()
    bvh = nc.dram_tensor("bv", [1, D], F32, kind="ExternalInput").ap()
    boh = nc.dram_tensor("bo", [1, D], F32, kind="ExternalInput").ap()
    projh = nc.dram_tensor("proj", [DH, NF], F32, kind="ExternalInput").ap()
    wgh = nc.dram_tensor("wg", [D, E], F32, kind="ExternalInput").ap()
    bgh = nc.dram_tensor("bg", [1, E], F32, kind="ExternalInput").ap()
    w1h = nc.dram_tensor("w1", [E, D, FF], F32, kind="ExternalInput").ap()
    b1h = nc.dram_tensor("b1", [E, FF], F32, kind="ExternalInput").ap()
    w2h = nc.dram_tensor("w2", [E, FF, D], F32, kind="ExternalInput").ap()
    b2h = nc.dram_tensor("b2", [E, D], F32, kind="ExternalInput").ap()
    gamh = nc.dram_tensor("gam", [1, D], F32, kind="ExternalInput").ap()
    beth = nc.dram_tensor("bet", [1, D], F32, kind="ExternalInput").ap()
    eyeh = nc.dram_tensor("eye", [128, 128], F32, kind="ExternalInput").ap()
    sel8h = nc.dram_tensor("sel8", [E, E * 128], F32, kind="ExternalInput").ap()
    outh = nc.dram_tensor("out", [D, T], F32, kind="ExternalOutput").ap()
    w1s = nc.dram_tensor("w1scr", [E, FC, 128, 2 * NCH * 128], BF16,
                         kind="Internal").ap()
    w2s = nc.dram_tensor("w2scr", [E, FC, 2, 128, 2, 512], BF16,
                         kind="Internal").ap()
    dbg = {}
    if debug:
        dbg["y1"] = nc.dram_tensor("dbg_y1", [D, T], F32,
                                   kind="ExternalOutput").ap()
        dbg["mix"] = nc.dram_tensor("dbg_mix", [D, T], F32,
                                    kind="ExternalOutput").ap()

    def sb(name, shape, dtype=F32):
        return nc.alloc_sbuf_tensor(name, shape, dtype).ap()

    REG = sb("reg", [128, NCH, T])
    BVS = REG[0:1, 0, 3072:4096].rearrange("p (o2 d) -> p o2 d", o2=1)
    KV = sb("kvsb", [128, 2, H, DH + 1])
    WST = sb("wst", [128, 2, NCH, 128])
    CPD = sb("cpd", [128, NF])
    NEG1 = sb("neg1", [128, NF])
    ONES = sb("ones", [128, 128])
    EYE = sb("eye_s", [128, 128])
    SEL8 = sb("sel8_s", [E, E * 128])
    B2S = sb("b2s", [E, D])
    BQ = sb("bq_s", [128, NCH])
    BK = sb("bk_s", [128, NCH])
    BO = sb("bo_s", [128, NCH])
    GAM = sb("gam_s", [128, NCH])
    BET = sb("bet_s", [128, NCH])
    B1S = sb("b1s", [128, E, FC])
    WGS = sb("wgs", [128, NCH, E])
    WGHB = sb("wghb", [128, NCH, E], BF16)
    WGLB = sb("wglb", [128, NCH, E], BF16)
    BGS = sb("bgs", [1, E])
    WTM = sb("wtm", [128, NSUB, E])
    WFM8 = sb("wfm8", [E, TT])
    GS = sb("gs", [128, 40])

    SCRB_W = 18432
    SCRB = sb("scrb", [128, SCRB_W], BF16)
    AR_W = nc.sbuf_bytes_remaining // 4 - 16
    SCR = sb("scr", [128, AR_W])

    def view_t(base_ap, width, off, shape):
        n = int(np.prod(shape))
        assert off + n <= width, (off, shape, width)
        a = base_ap[:, off:off + n]
        if len(shape) == 1:
            return a
        if len(shape) == 2:
            return a.rearrange("p (a b) -> p a b", a=shape[0])
        if len(shape) == 3:
            return a.rearrange("p (a b c) -> p a b c", a=shape[0], b=shape[1])
        return a.rearrange("p (a b c d) -> p a b c d", a=shape[0],
                           b=shape[1], c=shape[2])

    def view(off, shape):
        return view_t(SCR, AR_W, off, shape)

    def viewb(off, shape):
        return view_t(SCRB, SCRB_W, off, shape)

    o = [0]

    def take(n):
        r = o[0]
        o[0] += n
        return r

    o[0] = 0
    P1_XT = view(take(NCH * TT), (NCH, TT))
    _b = take(2 * NCH * 128)
    P1_WMV = view(_b, (2, NCH, 128))
    P1_KQ = view(_b, (4, TT))
    P1_VAUG = view(take(NSUB * 8 * (DH + 1)), (NSUB, 8, DH + 1))
    P1_SQ2 = view(take(TT), (TT,))
    P1_KP = view(take(NSUB * NF), (NSUB, NF))
    p1_end = o[0]
    o[0] = 0
    P2_XT = view(take(NCH * TT), (NCH, TT))
    P2_KQ = view(take(4 * TT), (4, TT))
    P2_ZIN = view(take(NCH * TT), (NCH, TT))
    P2_SQ2 = view(take(TT), (TT,))
    P2_QP = view(take(2 * TT), (2, TT))
    P2_STRIP = view(take(2 * TT), (2, TT))
    P2_SQS = P2_ZIN
    p2_end = o[0]
    o[0] = 0
    F_MIX = view(take(NCH * TT), (NCH, TT))
    F_SQS = view(take(NCH * TT), (NCH, TT))
    F_WBC = view(take(TT), (TT,))
    F_RELU = view(take(TT), (TT,))
    F_STRIP = view(take(2 * TT), (2, TT))
    f_end = o[0]
    assert max(p1_end, p2_end, f_end) <= AR_W, (p1_end, p2_end, f_end, AR_W)
    o[0] = 0
    YHB = viewb(take(NCH * TT), (NCH, TT))
    YLB = viewb(take(NCH * TT), (NCH, TT))
    HTH = viewb(take(FC * TT), (FC, TT))
    HTL = viewb(take(FC * TT), (FC, TT))
    WSB = viewb(take(2 * 2 * NCH * 128), (2, 2, NCH, 128))
    W2B = viewb(take(2 * 2 * 512), (2, 2, 512))
    fb_end = o[0]
    assert fb_end <= SCRB_W, (fb_end, SCRB_W)
    PPB = viewb(0, (2, 2, NCH, 128))

    PS = nc.alloc_psum_tensor("ps", [128, 8, 512], F32).ap()

    for s in ("sC", "sX", "sW0", "sW1", "sM0", "sM1", "sPE", "sDV", "sAC",
              "sOut", "sPP"):
        p.sem(s)

    guard = []

    def dma(out_ap, in_ap, sem, waits=(), slow=False):
        return p.op("sync",
                    lambda e, o_=out_ap, i_=in_ap, s_=slow: e.dma_start(
                        out=o_, in_=i_, allow_slow_non_contiguous=s_),
                    waits=list(waits) + guard, incs=((sem, 16),))

    def mm(out_ap, lhsT, rhs, start, stop, waits=(), inc=False):
        return p.op("tensor",
                    lambda e, o_=out_ap, l_=lhsT, r_=rhs, s0=start, s1=stop:
                    e.matmul(o_, l_, r_, start=s0, stop=s1),
                    waits=list(waits) + guard,
                    incs=((("sPE", 1),) if inc else ()))

    def vop(fn, waits=(), force=False):
        return p.op("vector", fn, waits=list(waits) + guard,
                    incs=(("sDV", 1),), force=force)

    def aop(fn, waits=(), force=False):
        return p.op("scalar", fn, waits=list(waits) + guard,
                    incs=(("sAC", 1),), force=force)

    def set_guard():
        guard.clear()
        guard.extend([("sPE", p.tot["sPE"]), ("sDV", p.tot["sDV"]),
                      ("sAC", p.tot["sAC"]), ("sOut", p.tot["sOut"])])

    ncw = [0]

    def cdma(out_ap, in_ap, slow=False):
        ncw[0] += 1
        return dma(out_ap, in_ap, "sC", slow=slow)

    cdma(CPD[0:DH, :], projh[:, :])
    cdma(CPD[DH:128, :], projh[:, :])
    cdma(EYE[:, :], eyeh[:, :])
    cdma(SEL8[:, :], sel8h[:, :])
    cdma(B2S[:, :], b2h[:, :])
    cdma(BQ[:, :], bqh.rearrange("o (c p) -> (o p) c", p=128), slow=True)
    cdma(BK[:, :], bkh.rearrange("o (c p) -> (o p) c", p=128), slow=True)
    cdma(BO[:, :], boh.rearrange("o (c p) -> (o p) c", p=128), slow=True)
    cdma(GAM[:, :], gamh.rearrange("o (c p) -> (o p) c", p=128), slow=True)
    cdma(BET[:, :], beth.rearrange("o (c p) -> (o p) c", p=128), slow=True)
    cdma(B1S[:, :, :], b1h.rearrange("e (fc p) -> p e fc", p=128), slow=True)
    cdma(WGS[:, :, :], wgh.rearrange("(c p) e -> p c e", p=128))
    cdma(BGS[:, :], bgh[:, :])
    cdma(BVS[0:1, 0, :], bvh[:, :])
    call = ("sC", 16 * ncw[0])
    t_scale = vop(lambda e: e.tensor_scalar_mul(CPD[:, :], CPD[:, :],
                                                float(C1)), waits=[call])
    vop(lambda e: e.memset(NEG1[:, :], float(NEGH)))
    vop(lambda e: e.memset(ONES[:, :], 1.0))
    twg = vop(lambda e: e.tensor_copy(out=WGHB[:, :, :], in_=WGS[:, :, :]),
              waits=[call])
    twg = vop(lambda e: e.tensor_sub(WGLB[:, :, :], WGS[:, :, :],
                                     WGHB[:, :, :]), waits=[twg], force=True)
    cready = [call, t_scale, twg]

    wst_free = [None, None]
    wst_ready = [None, None]
    wst_i = [0]

    def wst_load(in_ap, ap_dst=None):
        slot = wst_i[0] % 2
        wst_i[0] += 1
        dst = WST[:, slot, :, :] if ap_dst is None else ap_dst(slot)
        tok = dma(dst, in_ap, "sW0" if slot == 0 else "sW1",
                  waits=[wst_free[slot]])
        wst_ready[slot] = tok
        return slot

    mv_free = [None, None]
    mv_ready = [None, None]
    mv_i = [0]

    def mv_load(dst_ap, in_ap, extra=()):
        slot = mv_i[0] % 2
        mv_i[0] += 1
        tok = dma(dst_ap(slot), in_ap, "sM0" if slot == 0 else "sM1",
                  waits=[mv_free[slot]] + list(extra))
        mv_ready[slot] = tok
        return slot

    xt_state = {"free": None}

    def x_load(buf, t0):
        tok = dma(buf[:, :, :],
                  xh.rearrange("(c p) t -> p c t", p=128)[:, :, t0:t0 + TT],
                  "sX", waits=[xt_state["free"]])
        return tok

    bank_free = [None] * 8
    strip_free = [None]

    def _asl(w, dc, j0, j1):
        return w[dc, :, :].rearrange("p (c j) -> p c j", c=NCH)[:, :, j0:j1]
    w1slice = w1h.rearrange("e (c p) f -> p e c f", p=128)
    w2slice = w2h.rearrange("e (f p) d -> p e f d", p=128)

    PPF = [REG[:, 6, 3072:4096].rearrange("p (a b) -> p a b", a=NCH),
           REG[:, 7, 3072:4096].rearrange("p (a b) -> p a b", a=NCH)]
    pp_state = {"i": 0, "free": [None, None], "bfree": [None, None]}
    pp_slices = [("w1", e_i, fcg) for e_i in range(E) for fcg in range(FC)]
    pp_slices += [("w2", e_i, fcg) for e_i in range(E) for fcg in range(FC)]

    def emit_prepass(nslices):
        for _ in range(nslices):
            i = pp_state["i"]
            if i >= len(pp_slices):
                return
            pp_state["i"] = i + 1
            kind, e_i, fcg = pp_slices[i]
            slot = i % 2
            if kind == "w1":
                src = w1slice[:, e_i, :, fcg * 128:(fcg + 1) * 128]
            else:
                src = w2slice[:, e_i, fcg, :].rearrange(
                    "p (a b) -> p a b", a=NCH)
            tin = dma(PPF[slot][:, :, :], src, "sPP",
                      waits=[pp_state["free"][slot]])
            th = vop(lambda e, s_=slot: e.tensor_copy(
                out=PPB[:, s_, 0, :, :], in_=PPF[s_][:, :, :]),
                waits=[tin, pp_state["bfree"][slot]])
            tl = vop(lambda e, s_=slot: e.tensor_sub(
                PPB[:, s_, 1, :, :], PPF[s_][:, :, :], PPB[:, s_, 0, :, :]),
                waits=[th], force=True)
            pp_state["free"][slot] = tl
            if kind == "w1":
                dsth = w1s[e_i, fcg, :, 0:NCH * 128].rearrange(
                    "p (a b) -> p a b", a=NCH)
                dstl = w1s[e_i, fcg, :, NCH * 128:2 * NCH * 128].rearrange(
                    "p (a b) -> p a b", a=NCH)
                dma(dsth, PPB[:, slot, 0, :, :], "sPP", waits=[tl])
                tout = dma(dstl, PPB[:, slot, 1, :, :], "sPP", waits=[tl])
            else:
                for dh2 in range(2):
                    dma(w2s[e_i, fcg, dh2, :, 0, :],
                        PPB[:, slot, 0, :, :].rearrange(
                            "p b c -> p (b c)")[:, dh2 * 512:(dh2 + 1) * 512],
                        "sPP", waits=[tl])
                    tout = dma(w2s[e_i, fcg, dh2, :, 1, :],
                               PPB[:, slot, 1, :, :].rearrange(
                                   "p b c -> p (b c)")[:, dh2 * 512:
                                                       (dh2 + 1) * 512],
                               "sPP", waits=[tl])
            pp_state["bfree"][slot] = tout

    def layernorm(buf, sqs, strip, src_toks, affine, extra_sq_waits=()):
        tsq = aop(lambda e, b_=buf, q_=sqs: e.activation(
            out=q_[:, :, :], in_=b_[:, :, :], func=ACTF.Square, bias=f32(0.0),
            scale=f32(1.0)), waits=list(src_toks) + list(extra_sq_waits))
        t6 = None
        for c in range(NCH):
            t6 = mm(PS[0:1, 6, 0:TT], ONES[:, 0:1], buf[:, c, :], c == 0,
                    c == NCH - 1, waits=[bank_free[6]] + list(src_toks),
                    inc=(c == NCH - 1))
        t7 = None
        for c in range(NCH):
            t7 = mm(PS[0:1, 7, 0:TT], ONES[:, 0:1], sqs[:, c, :], c == 0,
                    c == NCH - 1, waits=[bank_free[7], tsq],
                    inc=(c == NCH - 1))
        tm = vop(lambda e, s_=strip: e.tensor_scalar_mul(
            s_[0:1, 0, :], PS[0:1, 6, 0:TT], float(1.0 / D)),
            waits=[t6, strip_free[0]])
        bank_free[6] = tm
        tmm = vop(lambda e, s_=strip: e.tensor_mul(
            s_[0:1, 1, :], s_[0:1, 0, :], s_[0:1, 0, :]),
            waits=[tm], force=True)
        tv = vop(lambda e, s_=strip: e.scalar_tensor_tensor(
            out=s_[0:1, 1, :], in0=PS[0:1, 7, 0:TT], scalar=float(1.0 / D),
            in1=s_[0:1, 1, :], op0=ALU.mult, op1=ALU.subtract),
            waits=[t7, tmm], force=True)
        bank_free[7] = tv
        ts = aop(lambda e, s_=strip: e.activation(
            out=s_[0:1, 1, :], in_=s_[0:1, 1, :], func=ACTF.Sqrt,
            bias=f32(LN_EPS), scale=f32(1.0)), waits=[tv])
        tr = vop(lambda e, s_=strip: e.reciprocal(s_[0:1, 1, :],
                                                  s_[0:1, 1, :]),
                 waits=[ts])
        tb4 = mm(PS[:, 4, 0:TT], ONES[0:1, 0:128], strip[0:1, 0, :], True,
                 True, waits=[bank_free[4], tm], inc=True)
        tb5 = mm(PS[:, 5, 0:TT], ONES[0:1, 0:128], strip[0:1, 1, :], True,
                 True, waits=[bank_free[5], tr], inc=True)
        strip_free[0] = tb5
        last = None
        for c in range(NCH):
            vop(lambda e, c_=c, b_=buf: e.tensor_sub(
                b_[:, c_, :], b_[:, c_, :], PS[:, 4, 0:TT]), waits=[tb4])
            last = vop(lambda e, c_=c, b_=buf: e.tensor_mul(
                b_[:, c_, :], b_[:, c_, :], PS[:, 5, 0:TT]), waits=[tb5])
            if affine:
                last = vop(lambda e, c_=c, b_=buf: e.tensor_scalar(
                    out=b_[:, c_, :], in0=b_[:, c_, :],
                    scalar1=GAM[:, c_:c_ + 1], scalar2=BET[:, c_:c_ + 1],
                    op0=ALU.mult, op1=ALU.add))
        bank_free[4] = last
        bank_free[5] = last
        return last

    vaug_free = None
    breg_free = []
    for s in range(2):
        set_guard()
        kvz = vop(lambda e: e.memset(KV[:, :, :, :], 0.0),
                  waits=(cready if s == 0 else ()))
        vones = vop(lambda e: e.memset(P1_VAUG[:, :, :, DH:DH + 1], 1.0))
        for ti in range(NTS):
            t0 = s * SEQ + ti * TT
            if s == 0:
                emit_prepass(16)
            xw = x_load(P1_XT, t0)
            kv_last_mm = None
            for half in range(2):
                vlast = None
                for oc in range(4):
                    oct_ = half * 4 + oc
                    ms = mv_load(lambda sl: P1_WMV[:, sl, :, :],
                                 _asl(wv, oct_, 0, 128),
                                 extra=breg_free if oc == 0 else ())
                    for tt in range(NSUB):
                        b = (oc * NSUB + tt) % 2
                        mm(PS[:, b, 0:128], ONES[0:1, 0:128],
                           BVS[0:1, 0, oct_ * 128:(oct_ + 1) * 128], True,
                           False,
                           waits=[bank_free[b], mv_ready[ms], xw] + cready)
                        tok = None
                        for c in range(NCH):
                            tok = mm(PS[:, b, 0:128],
                                     P1_XT[:, c, tt * 128:(tt + 1) * 128],
                                     P1_WMV[:, ms, c, :], False, c == NCH - 1,
                                     inc=(c == NCH - 1))
                        if tt == NSUB - 1:
                            mv_free[ms] = tok
                        vlast = vop(lambda e, b_=b, tt_=tt, q_=oc:
                                    e.tensor_copy(
                                        out=P1_VAUG[:, tt_,
                                                    q_ * 2:q_ * 2 + 2, 0:DH],
                                        in_=PS[:, b_, 0:128].rearrange(
                                            "p (a b) -> p a b", a=2)),
                                    waits=[tok, vaug_free])
                        bank_free[b] = vlast
                vdone = vlast
                evtoks = []
                for cl in range(4):
                    cglob = half * 4 + cl
                    ws = wst_load(_asl(wk, cglob, 0, 128))
                    b = 2 + (cl % 2)
                    tok = None
                    for c in range(NCH):
                        tok = mm(PS[:, b, 0:TT], WST[:, ws, c, :],
                                 P1_XT[:, c, :], c == 0, c == NCH - 1,
                                 waits=([bank_free[b], wst_ready[ws], xw,
                                         mv_free[0], mv_free[1]]
                                        if c == 0 else ()),
                                 inc=(c == NCH - 1))
                    wst_free[ws] = tok
                    ev = aop(lambda e, b_=b, cl_=cl, cg=cglob:
                             e.activation(out=P1_KQ[:, cl_, :],
                                          in_=PS[:, b_, 0:TT],
                                          func=ACTF.Identity,
                                          bias=BK[:, cg:cg + 1],
                                          scale=f32(1.0)),
                             waits=[tok])
                    evtoks.append(ev)
                    bank_free[b] = ev
                sq2_free = None
                tsq = None
                for cl in range(4):
                    cglob = half * 4 + cl
                    tsq = aop(lambda e, cl_=cl:
                              e.activation(out=P1_SQ2[:, :],
                                           in_=P1_KQ[:, cl_, :],
                                           func=ACTF.Square, bias=f32(0.0),
                                           scale=f32(1.0)),
                              waits=[evtoks[cl], sq2_free])
                    for dlt in range(2):
                        h = 2 * cglob + dlt
                        hloc = h - 8 * half
                        base = dlt * 64
                        eps_toks = []
                        for tt in range(NSUB):
                            b = tt % 2
                            mm(PS[:, b, 0:NF],
                               P1_KQ[base:base + DH, cl,
                                     tt * 128:(tt + 1) * 128],
                               CPD[base:base + DH, :], True, False,
                               waits=[bank_free[b], evtoks[cl]])
                            tok = mm(PS[:, b, 0:NF],
                                     P1_SQ2[base:base + DH,
                                            tt * 128:(tt + 1) * 128],
                                     NEG1[base:base + DH, 0:NF], False, True,
                                     waits=[tsq], inc=True)
                            sq2_free = tok
                            te = aop(lambda e, b_=b, tt_=tt:
                                     e.activation(out=P1_KP[:, tt_, :],
                                                  in_=PS[:, b_, 0:NF],
                                                  func=ACTF.Exp,
                                                  bias=f32(LNRATIO),
                                                  scale=f32(1.0)),
                                     waits=[tok])
                            bank_free[b] = te
                            tp = vop(lambda e, tt_=tt:
                                     e.tensor_scalar_add(P1_KP[:, tt_, :],
                                                         P1_KP[:, tt_, :],
                                                         float(EPSR)),
                                     waits=[te])
                            eps_toks.append(tp)
                        for fc2 in range(2):
                            b = 4 + fc2
                            tok = None
                            for tt in range(NSUB):
                                tok = mm(PS[:, b, 0:DH + 1],
                                         P1_KP[:, tt,
                                               fc2 * 128:(fc2 + 1) * 128],
                                         P1_VAUG[:, tt, hloc, :], tt == 0,
                                         tt == NSUB - 1,
                                         waits=([bank_free[b], eps_toks[tt],
                                                 vdone, kvz, vones]
                                                if tt == 0 else
                                                [eps_toks[tt]]),
                                         inc=(tt == NSUB - 1))
                            kv_last_mm = tok
                            tkv = vop(lambda e, b_=b, f_=fc2, h_=h:
                                      e.tensor_add(KV[:, f_, h_, :],
                                                   KV[:, f_, h_, :],
                                                   PS[:, b_, 0:DH + 1]),
                                      waits=[tok])
                            bank_free[b] = tkv
                vaug_free = kv_last_mm
                breg_free = [kv_last_mm, tsq]
            xt_state["free"] = kv_last_mm
        kv_done_dv = ("sDV", p.tot["sDV"])
        set_guard()
        for ti in range(NTS):
            t0 = s * SEQ + ti * TT
            if s == 0:
                emit_prepass(16)
            xw = x_load(P2_XT, t0)
            zin_last = None
            sq2_free = None
            kq_free = None
            for half in range(2):
                evtoks = []
                for cl in range(4):
                    cglob = half * 4 + cl
                    ws = wst_load(_asl(wq, cglob, 0, 128))
                    b = 2 + (cl % 2)
                    tok = None
                    for c in range(NCH):
                        tok = mm(PS[:, b, 0:TT], WST[:, ws, c, :],
                                 P2_XT[:, c, :], c == 0, c == NCH - 1,
                                 waits=([bank_free[b], wst_ready[ws], xw]
                                        if c == 0 else ()),
                                 inc=(c == NCH - 1))
                    wst_free[ws] = tok
                    ev = aop(lambda e, b_=b, cl_=cl, cg=cglob:
                             e.activation(out=P2_KQ[:, cl_, :],
                                          in_=PS[:, b_, 0:TT],
                                          func=ACTF.Identity,
                                          bias=BQ[:, cg:cg + 1],
                                          scale=f32(1.0)),
                             waits=[tok, kq_free])
                    evtoks.append(ev)
                    bank_free[b] = ev
                for cl in range(4):
                    cglob = half * 4 + cl
                    tsq = aop(lambda e, cl_=cl:
                              e.activation(out=P2_SQ2[:, :],
                                           in_=P2_KQ[:, cl_, :],
                                           func=ACTF.Square, bias=f32(0.0),
                                           scale=f32(1.0)),
                              waits=[evtoks[cl], sq2_free])
                    for dlt in range(2):
                        h = 2 * cglob + dlt
                        base = dlt * 64
                        eps_toks = []
                        for fc2 in range(2):
                            b = fc2
                            mm(PS[:, b, 0:TT],
                               CPD[base:base + DH, fc2 * 128:(fc2 + 1) * 128],
                               P2_KQ[base:base + DH, cl, :], True, False,
                               waits=[bank_free[b], evtoks[cl]])
                            tok = mm(PS[:, b, 0:TT],
                                     NEG1[base:base + DH, 0:128],
                                     P2_SQ2[base:base + DH, :], False, True,
                                     waits=[tsq], inc=True)
                            sq2_free = tok
                            te = aop(lambda e, b_=b, f_=fc2:
                                     e.activation(out=P2_QP[:, f_, :],
                                                  in_=PS[:, b_, 0:TT],
                                                  func=ACTF.Exp,
                                                  bias=f32(LNRATIO),
                                                  scale=f32(1.0)),
                                     waits=[tok])
                            bank_free[b] = te
                            tp = vop(lambda e, f_=fc2:
                                     e.tensor_scalar_add(P2_QP[:, f_, :],
                                                         P2_QP[:, f_, :],
                                                         float(EPSR)),
                                     waits=[te])
                            eps_toks.append(tp)
                        ba = 4 + (h % 2)
                        bn = 6 + (h % 2)
                        tok = None
                        for fc2 in range(2):
                            tok = mm(PS[base:base + DH, ba, 0:TT],
                                     KV[:, fc2, h, 0:DH],
                                     P2_QP[:, fc2, :], fc2 == 0, fc2 == 1,
                                     waits=([bank_free[ba], eps_toks[0],
                                             eps_toks[1], kv_done_dv]
                                            if fc2 == 0 else ()),
                                     inc=(fc2 == 1))
                        qkv_mm = tok
                        tok = None
                        for fc2 in range(2):
                            tok = mm(PS[0:1, bn, 0:TT],
                                     KV[:, fc2, h, DH:DH + 1],
                                     P2_QP[:, fc2, :], fc2 == 0, fc2 == 1,
                                     waits=([bank_free[bn]] if fc2 == 0
                                            else ()),
                                     inc=(fc2 == 1))
                        nrm_mm = tok
                        tn = vop(lambda e, b_=bn:
                                 e.tensor_scalar_add(P2_STRIP[0:1, 0, :],
                                                     PS[0:1, b_, 0:TT],
                                                     float(1e-6)),
                                 waits=[nrm_mm, strip_free[0]])
                        tn = vop(lambda e: e.reciprocal(P2_STRIP[0:1, 1, :],
                                                        P2_STRIP[0:1, 0, :]),
                                 waits=[tn], force=True)
                        tb = mm(PS[:, bn, 0:TT], ONES[0:1, 0:128],
                                P2_STRIP[0:1, 1, :], True, True,
                                waits=[tn], inc=True)
                        strip_free[0] = tb
                        vop(lambda e, b_=ba, bs=base, c2=cglob:
                            e.tensor_copy(out=P2_ZIN[bs:bs + DH, c2, :],
                                          in_=PS[bs:bs + DH, b_, 0:TT]),
                            waits=[qkv_mm])
                        zin_last = vop(
                            lambda e, bn_=bn, bs=base, c2=cglob:
                            e.tensor_mul(P2_ZIN[bs:bs + DH, c2, :],
                                         P2_ZIN[bs:bs + DH, c2, :],
                                         PS[bs:bs + DH, bn_, 0:TT]),
                            waits=[tb])
                        bank_free[ba] = zin_last
                        bank_free[bn] = zin_last
                    kq_free = ("sPE", p.tot["sPE"])
            ev_toks = []
            wo_mm_last = None
            for dc in range(NCH):
                ws = wst_load(_asl(wo, dc, 0, 128))
                b = 2 + (dc % 2)
                tok = None
                for c in range(NCH):
                    tok = mm(PS[:, b, 0:TT], WST[:, ws, c, :],
                             P2_ZIN[:, c, :], c == 0, c == NCH - 1,
                             waits=([bank_free[b], wst_ready[ws], zin_last]
                                    if c == 0 else ()),
                             inc=(c == NCH - 1))
                wst_free[ws] = tok
                wo_mm_last = tok
                ev = vop(lambda e, b_=b, dc_=dc:
                         e.scalar_tensor_tensor(
                             out=P2_XT[:, dc_, :], in0=PS[:, b_, 0:TT],
                             scalar=BO[:, dc_:dc_ + 1],
                             in1=P2_XT[:, dc_, :],
                             op0=ALU.add, op1=ALU.add),
                         waits=[tok])
                ev_toks.append(ev)
                bank_free[b] = ev
            layernorm(P2_XT, P2_SQS, P2_STRIP, ev_toks, affine=False,
                      extra_sq_waits=[wo_mm_last])
            cplast = None
            for c in range(NCH):
                cplast = vop(lambda e, c_=c, t0_=t0: e.tensor_copy(
                    out=REG[:, c_, t0_:t0_ + TT], in_=P2_XT[:, c_, :]))
            xt_state["free"] = cplast
            if debug:
                dma(dbg["y1"].rearrange("(c p) t -> p c t", p=128)
                    [:, :, t0:t0 + TT], REG[:, :, t0:t0 + TT], "sOut",
                    waits=[cplast])

    ppdone = ("sPP", p.tot["sPP"])
    mix_free = None
    ht_free = None
    yb_free = None
    for ti in range(NT):
        set_guard()
        if ti == 0:
            guard.append(ppdone)
        t0 = ti * TT
        regslot = REG[:, :, t0:t0 + TT]
        ylast = None
        for c in range(NCH):
            th = vop(lambda e, c_=c, r_=regslot: e.tensor_copy(
                out=YHB[:, c_, :], in_=r_[:, c_, :]), waits=[yb_free])
            ylast = vop(lambda e, c_=c, r_=regslot: e.tensor_sub(
                YLB[:, c_, :], r_[:, c_, :], YHB[:, c_, :]),
                waits=[th], force=True)
        wtoks = []
        for tt in range(NSUB):
            b = 6 + (tt % 2)
            mm(PS[:, b, 0:E], ONES[0:1, 0:128], BGS[0:1, :], True, False,
               waits=[bank_free[b], ylast])
            tok = None
            for c in range(NCH):
                sl = slice(tt * 128, (tt + 1) * 128)
                mm(PS[:, b, 0:E], YHB[:, c, sl], WGHB[:, c, :], False, False)
                mm(PS[:, b, 0:E], YHB[:, c, sl], WGLB[:, c, :], False, False)
                tok = mm(PS[:, b, 0:E], YLB[:, c, sl], WGHB[:, c, :], False,
                         c == NCH - 1, inc=(c == NCH - 1))
            t1 = vop(lambda e, b_=b: e.reduce_max(GS[:, 24:25],
                                                  PS[:, b_, 0:E], axis=AX.X),
                     waits=[tok])
            teq = vop(lambda e, b_=b: e.tensor_scalar(
                out=GS[:, 0:8], in0=PS[:, b_, 0:E], scalar1=GS[:, 24:25],
                scalar2=None, op0=ALU.is_equal), waits=[t1], force=True)
            tst = vop(lambda e, b_=b: e.scalar_tensor_tensor(
                out=GS[:, 0:8], in0=GS[:, 0:8], scalar=float(-1e30),
                in1=PS[:, b_, 0:E], op0=ALU.mult, op1=ALU.add),
                waits=[teq], force=True)
            tm2 = vop(lambda e: e.reduce_max(GS[:, 26:27], GS[:, 0:8],
                                             axis=AX.X), waits=[tst],
                      force=True)
            tneg = vop(lambda e: e.tensor_scalar_mul(GS[:, 25:26],
                                                     GS[:, 24:25],
                                                     float(-1.0)))
            tgem = vop(lambda e, b_=b: e.tensor_scalar(
                out=GS[:, 8:16], in0=PS[:, b_, 0:E], scalar1=GS[:, 26:27],
                scalar2=None, op0=ALU.is_ge), waits=[tm2], force=True)
            tel = aop(lambda e, b_=b: e.activation(
                out=GS[:, 16:24], in_=PS[:, b_, 0:E], func=ACTF.Exp,
                bias=GS[:, 25:26], scale=f32(1.0)), waits=[tneg, tgem])
            tdn = aop(lambda e: e.activation(
                out=GS[:, 27:28], in_=GS[:, 26:27], func=ACTF.Exp,
                bias=GS[:, 25:26], scale=f32(1.0)))
            bank_free[b] = [tgem, tel]
            tda = vop(lambda e: e.tensor_scalar_add(GS[:, 27:28],
                                                    GS[:, 27:28],
                                                    float(1.0)), waits=[tdn])
            trdn = vop(lambda e: e.reciprocal(GS[:, 28:29], GS[:, 27:28]),
                       waits=[tda], force=True)
            vop(lambda e, tt_=tt: e.tensor_mul(WTM[:, tt_, :], GS[:, 16:24],
                                               GS[:, 8:16]),
                waits=[tel, tgem], force=True)
            tw = vop(lambda e, tt_=tt: e.tensor_scalar_mul(
                WTM[:, tt_, :], WTM[:, tt_, :], GS[:, 28:29]),
                waits=[trdn], force=True)
            wtoks.append(tw)
        trtoks = []
        for tt in range(NSUB):
            tok = p.op("tensor",
                       lambda e, tt_=tt: e.transpose(
                           PS[0:E, 7, tt_ * 128:(tt_ + 1) * 128],
                           WTM[:, tt_, :], EYE[:, :]),
                       waits=[wtoks[tt], bank_free[7]] + guard,
                       incs=(("sPE", 1),))
            trtoks.append(tok)
        tw8 = vop(lambda e: e.tensor_copy(out=WFM8[:, :],
                                          in_=PS[0:E, 7, 0:TT]),
                  waits=trtoks)
        bank_free[7] = tw8
        eo_ev_last = None
        for e_i in range(E):
            twb = mm(PS[:, 6, 0:TT], SEL8[:, e_i * 128:(e_i + 1) * 128],
                     WFM8[:, :], True, True,
                     waits=[tw8, bank_free[6]], inc=True)
            twbc = vop(lambda e: e.tensor_copy(out=F_WBC[:, :],
                                               in_=PS[:, 6, 0:TT]),
                       waits=[twb])
            bank_free[6] = twbc
            lim_toks = []
            for fcg in range(FC):
                ws = wst_load(w1s[e_i, fcg, :, :].rearrange('p (l c j) -> p l c j', l=2, c=NCH),
                              ap_dst=lambda sl: WSB[:, sl, :, :, :])
                b = fcg % 2
                tok = None
                for c in range(NCH):
                    mm(PS[:, b, 0:TT], WSB[:, ws, 0, c, :], YHB[:, c, :],
                       c == 0, False,
                       waits=([bank_free[b], wst_ready[ws], ylast]
                              if c == 0 else ()))
                    mm(PS[:, b, 0:TT], WSB[:, ws, 0, c, :], YLB[:, c, :],
                       False, False)
                    tok = mm(PS[:, b, 0:TT], WSB[:, ws, 1, c, :],
                             YHB[:, c, :], False, c == NCH - 1,
                             inc=(c == NCH - 1))
                wst_free[ws] = tok
                trl = aop(lambda e, b_=b, fc_=fcg, e__=e_i:
                          e.activation(out=F_RELU[:, :], in_=PS[:, b_, 0:TT],
                                       func=ACTF.Relu,
                                       bias=B1S[:, e__, fc_:fc_ + 1],
                                       scale=f32(1.0)),
                          waits=[tok, ht_free,
                                 lim_toks[-1] if lim_toks else None])
                bank_free[b] = trl
                tmu = vop(lambda e: e.tensor_mul(
                    F_RELU[:, :], F_RELU[:, :], F_WBC[:, :]),
                    waits=[trl, twbc])
                thh = vop(lambda e, fc_=fcg: e.tensor_copy(
                    out=HTH[:, fc_, :], in_=F_RELU[:, :]),
                    waits=[tmu], force=True)
                tll = vop(lambda e, fc_=fcg: e.tensor_sub(
                    HTL[:, fc_, :], F_RELU[:, :], HTH[:, fc_, :]),
                    waits=[thh], force=True)
                lim_toks.append(tll)
            for dh2 in range(2):
                gtoks = [None] * 4
                if e_i == 0:
                    for dl in range(4):
                        dc = dh2 * 4 + dl
                        gtoks[dl] = mm(PS[:, 2 + dl, 0:TT],
                                       B2S[:, dc * 128:(dc + 1) * 128],
                                       WFM8[:, :], True, False,
                                       waits=[bank_free[2 + dl], tw8])
                for fcg in range(FC):
                    wm = mv_load(lambda sl: W2B[:, sl, :, :],
                                 w2s[e_i, fcg, dh2, :, :, :])
                    for dl in range(4):
                        st = (fcg == 0) and (e_i != 0)
                        sp = (fcg == FC - 1)
                        w = [mv_ready[wm], lim_toks[fcg]]
                        if fcg == 0 and e_i != 0:
                            w.append(bank_free[2 + dl])
                        sl = slice(dl * 128, (dl + 1) * 128)
                        mm(PS[:, 2 + dl, 0:TT], W2B[:, wm, 0, sl],
                           HTH[:, fcg, :], st, False, waits=w)
                        mm(PS[:, 2 + dl, 0:TT], W2B[:, wm, 0, sl],
                           HTL[:, fcg, :], False, False)
                        gtoks[dl] = mm(PS[:, 2 + dl, 0:TT],
                                       W2B[:, wm, 1, sl],
                                       HTH[:, fcg, :], False, sp,
                                       inc=(sp or dl == 3))
                    mv_free[wm] = gtoks[3]
                for dl in range(4):
                    dc = dh2 * 4 + dl
                    if e_i == 0:
                        ev = vop(lambda e, dl_=dl, dc_=dc: e.tensor_copy(
                            out=F_MIX[:, dc_, :], in_=PS[:, 2 + dl_, 0:TT]),
                            waits=[gtoks[dl], mix_free])
                    else:
                        ev = vop(lambda e, dl_=dl, dc_=dc: e.tensor_add(
                            F_MIX[:, dc_, :], F_MIX[:, dc_, :],
                            PS[:, 2 + dl_, 0:TT]),
                            waits=[gtoks[dl]])
                    bank_free[2 + dl] = ev
                    eo_ev_last = ev
            ht_free = ("sPE", p.tot["sPE"])
        yb_free = ("sPE", p.tot["sPE"])
        add_toks = []
        for c in range(NCH):
            add_toks.append(vop(lambda e, c_=c, r_=regslot: e.tensor_add(
                F_MIX[:, c_, :], F_MIX[:, c_, :], r_[:, c_, :]),
                waits=[eo_ev_last]))
        layernorm(F_MIX, F_SQS, F_STRIP, [add_toks[-1]], affine=True,
                  extra_sq_waits=[ht_free])
        add_toks = []
        for c in range(NCH):
            add_toks.append(vop(lambda e, c_=c, r_=regslot: e.tensor_add(
                F_MIX[:, c_, :], F_MIX[:, c_, :], r_[:, c_, :])))
        lnlast = layernorm(F_MIX, F_SQS, F_STRIP, [add_toks[-1]],
                           affine=False)
        if debug:
            dma(dbg["mix"].rearrange("(c p) t -> p c t", p=128)
                [:, :, t0:t0 + TT], F_MIX[:, :, :], "sOut", waits=[lnlast])
        dma(outh.rearrange("(c p) t -> p c t", p=128)[:, :, t0:t0 + TT],
            F_MIX[:, :, :], "sOut", waits=[lnlast])
        mix_free = ("sOut", p.tot["sOut"])

    p.op("sync", None, waits=[("sOut", p.tot["sOut"])])

    import contextlib
    es = contextlib.ExitStack()
    sems = {s: es.enter_context(nc.semaphore(s)) for s in p.tot}
    block = es.enter_context(nc.Block())

    def emit(eng_name):
        def body(eng):
            for fn, waits, incs in p.items[eng_name]:
                for s, v in waits:
                    eng.wait_ge(sems[s], v)
                if fn is None:
                    continue
                instr = fn(eng)
                for s, d in incs:
                    instr.then_inc(sems[s], d)
        return body

    block.sync(emit("sync"))
    block.tensor(emit("tensor"))
    block.vector(emit("vector"))
    block.scalar(emit("scalar"))
    block.gpsimd(emit("gpsimd"))
    es.close()
    return nc


def _wre(w):
    w = np.asarray(w, np.float32).reshape(NCH, 128, NCH, 128)
    return np.ascontiguousarray(w.transpose(2, 1, 0, 3).reshape(NCH, 128, D))


def make_inmaps(inputs):
    q = np.asarray(inputs["L0_attn_queries"], np.float32)
    base = {
        "wq": _wre(inputs["L0_attn_Wq"]),
        "wk": _wre(inputs["L0_attn_Wk"]),
        "wv": _wre(inputs["L0_attn_Wv"]),
        "wo": _wre(inputs["L0_attn_Wo"]),
        "bq": np.asarray(inputs["L0_attn_bq"], np.float32).reshape(1, D),
        "bk": np.asarray(inputs["L0_attn_bk"], np.float32).reshape(1, D),
        "bv": np.asarray(inputs["L0_attn_bv"], np.float32).reshape(1, D),
        "bo": np.asarray(inputs["L0_attn_bo"], np.float32).reshape(1, D),
        "proj": np.ascontiguousarray(inputs["L0_attn_proj"], np.float32),
        "wg": np.ascontiguousarray(inputs["L0_ffn_Wg"], np.float32),
        "bg": np.asarray(inputs["L0_ffn_bg"], np.float32).reshape(1, E),
        "w1": np.ascontiguousarray(inputs["L0_ffn_W1"], np.float32),
        "b1": np.ascontiguousarray(inputs["L0_ffn_b1"], np.float32),
        "w2": np.ascontiguousarray(inputs["L0_ffn_W2"], np.float32),
        "b2": np.ascontiguousarray(inputs["L0_ffn_b2"], np.float32),
        "gam": np.asarray(inputs["L0_ffn_gamma"], np.float32).reshape(1, D),
        "bet": np.asarray(inputs["L0_ffn_beta"], np.float32).reshape(1, D),
        "eye": np.eye(128, dtype=np.float32),
        "sel8": np.kron(np.eye(E, dtype=np.float32),
                        np.ones((1, 128), np.float32)),
    }
    in_maps = []
    for c in range(8):
        m = dict(base)
        m["x"] = np.ascontiguousarray(q[2 * c:2 * c + 2].reshape(T, D).T)
        in_maps.append(m)
    return in_maps


def kernel(**inputs):
    in_maps = make_inmaps(inputs)
    nc = build_program()
    res = run_bass_kernel_spmd(nc, in_maps, core_ids=list(range(8)))
    outs = [np.asarray(r["out"]) for r in res.results]
    full = np.stack([o.T.reshape(2, SEQ, D) for o in outs])
    return np.ascontiguousarray(full.reshape(16, SEQ, D), dtype=np.float32)


# version 08
# speedup vs baseline: 1.2711x; 1.2614x over previous
import numpy as np
import concourse.bass as bass
import concourse.mybir as mybir
from concourse.bass_utils import run_bass_kernel_spmd

F32 = mybir.dt.float32
BF16 = mybir.dt.bfloat16
ALU = mybir.AluOpType
ACTF = mybir.ActivationFunctionType
AX = mybir.AxisListType
f32 = np.float32

D = 1024
NCH = 8
H = 16
DH = 64
NF = 256
E = 8
FF = 2048
FC = 16
SEQ = 2048
T = 4096
TT = 512
NTS = SEQ // TT
NT = T // TT
NSUB = TT // 128

C1 = 64.0 ** -0.25
NEGH = -(C1 * C1) / 2.0
LNRATIO = float(np.log(1.0 / np.sqrt(NF)))
EPSR = 1e-6 / 16.0
LN_EPS = 1e-5

ENGINES = ("sync", "tensor", "vector", "scalar", "gpsimd")


class Prog:
    def __init__(self):
        self.items = {e: [] for e in ENGINES}
        self.tot = {}
        self.self_contrib = {e: {} for e in ENGINES}
        self.last_wait = {e: {} for e in ENGINES}

    def sem(self, name):
        self.tot.setdefault(name, 0)

    def op(self, eng, fn, waits=(), incs=(), force=False):
        w = []
        flat = []
        for tok in waits:
            if tok is None:
                continue
            if isinstance(tok, list):
                flat.extend(t for t in tok if t is not None)
            else:
                flat.append(tok)
        for tok in flat:
            s, v = tok
            assert v <= self.tot[s], (s, v, self.tot[s])
            if v <= self.last_wait[eng].get(s, 0):
                continue
            if not force and v <= self.self_contrib[eng].get(s, 0):
                continue
            self.last_wait[eng][s] = v
            w.append((s, v))
        out = None
        for s, d in incs:
            self.tot[s] += d
            self.self_contrib[eng][s] = self.tot[s]
            out = (s, self.tot[s])
        self.items[eng].append((fn, tuple(w), tuple(incs)))
        return out


def build_program(debug=False):
    nc = bass.Bass("TRN2", target_bir_lowering=False, debug=False,
                   detect_race_conditions=False, dynamic_dma_scratch_size=2048)
    p = Prog()

    xh = nc.dram_tensor("x", [D, T], F32, kind="ExternalInput").ap()
    wq = nc.dram_tensor("wq", [NCH, 128, D], F32, kind="ExternalInput").ap()
    wk = nc.dram_tensor("wk", [NCH, 128, D], F32, kind="ExternalInput").ap()
    wv = nc.dram_tensor("wv", [NCH, 128, D], F32, kind="ExternalInput").ap()
    wo = nc.dram_tensor("wo", [NCH, 128, D], F32, kind="ExternalInput").ap()
    bqh = nc.dram_tensor("bq", [1, D], F32, kind="ExternalInput").ap()
    bkh = nc.dram_tensor("bk", [1, D], F32, kind="ExternalInput").ap()
    bvh = nc.dram_tensor("bv", [1, D], F32, kind="ExternalInput").ap()
    boh = nc.dram_tensor("bo", [1, D], F32, kind="ExternalInput").ap()
    projh = nc.dram_tensor("proj", [DH, NF], F32, kind="ExternalInput").ap()
    wgh = nc.dram_tensor("wg", [D, E], F32, kind="ExternalInput").ap()
    bgh = nc.dram_tensor("bg", [1, E], F32, kind="ExternalInput").ap()
    w1h = nc.dram_tensor("w1", [E, D, FF], F32, kind="ExternalInput").ap()
    b1h = nc.dram_tensor("b1", [E, FF], F32, kind="ExternalInput").ap()
    w2h = nc.dram_tensor("w2", [E, FF, D], F32, kind="ExternalInput").ap()
    b2h = nc.dram_tensor("b2", [E, D], F32, kind="ExternalInput").ap()
    gamh = nc.dram_tensor("gam", [1, D], F32, kind="ExternalInput").ap()
    beth = nc.dram_tensor("bet", [1, D], F32, kind="ExternalInput").ap()
    eyeh = nc.dram_tensor("eye", [128, 128], F32, kind="ExternalInput").ap()
    sel8h = nc.dram_tensor("sel8", [E, E * 128], F32, kind="ExternalInput").ap()
    outh = nc.dram_tensor("out", [D, T], F32, kind="ExternalOutput").ap()
    w1s = nc.dram_tensor("w1scr", [E, FC, 128, 2 * NCH * 128], BF16,
                         kind="Internal").ap()
    w2s = nc.dram_tensor("w2scr", [E, FC, 2, 128, 2, 512], BF16,
                         kind="Internal").ap()
    dbg = {}
    if debug:
        dbg["y1"] = nc.dram_tensor("dbg_y1", [D, T], F32,
                                   kind="ExternalOutput").ap()
        dbg["mix"] = nc.dram_tensor("dbg_mix", [D, T], F32,
                                    kind="ExternalOutput").ap()

    def sb(name, shape, dtype=F32):
        return nc.alloc_sbuf_tensor(name, shape, dtype).ap()

    REG = sb("reg", [128, NCH, SEQ])
    KV = sb("kvsb", [128, 2, H, DH + 1])
    WST = sb("wst", [128, 2, NCH, 128])
    CPD = sb("cpd", [128, NF])
    NEG1 = sb("neg1", [128, NF])
    ONES = sb("ones", [128, 128])
    EYE = sb("eye_s", [128, 128])
    SEL8 = sb("sel8_s", [E, E * 128])
    B2S = sb("b2s", [E, D])
    BQ = sb("bq_s", [128, NCH])
    BK = sb("bk_s", [128, NCH])
    BO = sb("bo_s", [128, NCH])
    GAM = sb("gam_s", [128, NCH])
    BET = sb("bet_s", [128, NCH])
    B1S = sb("b1s", [128, E, FC])
    WGS = sb("wgs", [128, NCH, E])
    WGHB = sb("wghb", [128, NCH, E], BF16)
    WGLB = sb("wglb", [128, NCH, E], BF16)
    BGS = sb("bgs", [1, E])
    WTM = sb("wtm", [128, NSUB, E])
    WFM8 = sb("wfm8", [E, TT])
    GS = sb("gs", [128, 40])

    SCRB_W = 30720
    SCRB = sb("scrb", [128, SCRB_W], BF16)
    AR_W = nc.sbuf_bytes_remaining // 4 - 16
    SCR = sb("scr", [128, AR_W])

    def view_t(base_ap, width, off, shape):
        n = int(np.prod(shape))
        assert off + n <= width, (off, shape, width)
        a = base_ap[:, off:off + n]
        if len(shape) == 1:
            return a
        if len(shape) == 2:
            return a.rearrange("p (a b) -> p a b", a=shape[0])
        if len(shape) == 3:
            return a.rearrange("p (a b c) -> p a b c", a=shape[0], b=shape[1])
        return a.rearrange("p (a b c d) -> p a b c d", a=shape[0],
                           b=shape[1], c=shape[2])

    def view(off, shape):
        return view_t(SCR, AR_W, off, shape)

    def viewb(off, shape):
        return view_t(SCRB, SCRB_W, off, shape)

    o = [0]

    def take(n):
        r = o[0]
        o[0] += n
        return r

    o[0] = 0
    P1_XT = view(take(NCH * TT), (NCH, TT))
    _b = take(2 * NCH * 128)
    P1_WMV = view(_b, (2, NCH, 128))
    P1_KQ = view(_b, (4, TT))
    P1_VAUG = view(take(NSUB * 8 * (DH + 1)), (NSUB, 8, DH + 1))
    P1_SQ2 = view(take(TT), (TT,))
    P1_KP = view(take(NSUB * NF), (NSUB, NF))
    p1_end = o[0]
    o[0] = 0
    P2_XT = view(take(NCH * TT), (NCH, TT))
    P2_KQ = view(take(4 * TT), (4, TT))
    P2_ZIN = view(take(NCH * TT), (NCH, TT))
    P2_SQ2 = view(take(TT), (TT,))
    P2_QP = view(take(2 * TT), (2, TT))
    P2_STRIP = view(take(2 * TT), (2, TT))
    P2_SQS = P2_ZIN
    p2_end = o[0]
    o[0] = 0
    F_MIX = view(take(NCH * TT), (NCH, TT))
    F_SQS = view(take(NCH * TT), (NCH, TT))
    F_WBC = view(take(TT), (TT,))
    F_RELU = view(take(TT), (TT,))
    F_STRIP = view(take(2 * TT), (2, TT))
    f_end = o[0]
    _aux = max(p1_end, p2_end, f_end)
    PPF = [view(_aux, (NCH, 128)), view(_aux + 1024, (NCH, 128))]
    _bvs = _aux + 2048
    BVS = SCR[0:1, _bvs:_bvs + D].rearrange("p (o2 d) -> p o2 d", o2=1)
    aux_end = _bvs + D
    assert aux_end <= AR_W, (aux_end, AR_W)
    o[0] = 0
    YHB = viewb(take(NCH * TT), (NCH, TT))
    YLB = viewb(take(NCH * TT), (NCH, TT))
    HTH = viewb(take(FC * TT), (FC, TT))
    HTL = viewb(take(FC * TT), (FC, TT))
    WSB = viewb(take(2 * 2 * NCH * 128), (2, 2, NCH, 128))
    W2B = viewb(take(2 * 2 * 512), (2, 2, 512))
    fb_end = o[0]
    assert fb_end <= SCRB_W, (fb_end, SCRB_W)
    PPB = viewb(0, (2, 2, NCH, 128))

    PS = nc.alloc_psum_tensor("ps", [128, 8, 512], F32).ap()

    for s in ("sC", "sX", "sW0", "sW1", "sM0", "sM1", "sPE", "sDV", "sAC",
              "sOut", "sPP"):
        p.sem(s)

    guard = []

    def dma(out_ap, in_ap, sem, waits=(), slow=False):
        return p.op("sync",
                    lambda e, o_=out_ap, i_=in_ap, s_=slow: e.dma_start(
                        out=o_, in_=i_, allow_slow_non_contiguous=s_),
                    waits=list(waits) + guard, incs=((sem, 16),))

    def mm(out_ap, lhsT, rhs, start, stop, waits=(), inc=False):
        return p.op("tensor",
                    lambda e, o_=out_ap, l_=lhsT, r_=rhs, s0=start, s1=stop:
                    e.matmul(o_, l_, r_, start=s0, stop=s1),
                    waits=list(waits) + guard,
                    incs=((("sPE", 1),) if inc else ()))

    def vop(fn, waits=(), force=False):
        return p.op("vector", fn, waits=list(waits) + guard,
                    incs=(("sDV", 1),), force=force)

    def aop(fn, waits=(), force=False):
        return p.op("scalar", fn, waits=list(waits) + guard,
                    incs=(("sAC", 1),), force=force)

    def set_guard():
        guard.clear()
        guard.extend([("sPE", p.tot["sPE"]), ("sDV", p.tot["sDV"]),
                      ("sAC", p.tot["sAC"]), ("sOut", p.tot["sOut"])])

    ncw = [0]

    def cdma(out_ap, in_ap, slow=False):
        ncw[0] += 1
        return dma(out_ap, in_ap, "sC", slow=slow)

    cdma(CPD[0:DH, :], projh[:, :])
    cdma(CPD[DH:128, :], projh[:, :])
    cdma(EYE[:, :], eyeh[:, :])
    cdma(SEL8[:, :], sel8h[:, :])
    cdma(B2S[:, :], b2h[:, :])
    cdma(BQ[:, :], bqh.rearrange("o (c p) -> (o p) c", p=128), slow=True)
    cdma(BK[:, :], bkh.rearrange("o (c p) -> (o p) c", p=128), slow=True)
    cdma(BO[:, :], boh.rearrange("o (c p) -> (o p) c", p=128), slow=True)
    cdma(GAM[:, :], gamh.rearrange("o (c p) -> (o p) c", p=128), slow=True)
    cdma(BET[:, :], beth.rearrange("o (c p) -> (o p) c", p=128), slow=True)
    cdma(B1S[:, :, :], b1h.rearrange("e (fc p) -> p e fc", p=128), slow=True)
    cdma(WGS[:, :, :], wgh.rearrange("(c p) e -> p c e", p=128))
    cdma(BGS[:, :], bgh[:, :])
    cdma(BVS[0:1, 0, :], bvh[:, :])
    call = ("sC", 16 * ncw[0])
    t_scale = vop(lambda e: e.tensor_scalar_mul(CPD[:, :], CPD[:, :],
                                                float(C1)), waits=[call])
    vop(lambda e: e.memset(NEG1[:, :], float(NEGH)))
    vop(lambda e: e.memset(ONES[:, :], 1.0))
    twg = vop(lambda e: e.tensor_copy(out=WGHB[:, :, :], in_=WGS[:, :, :]),
              waits=[call])
    twg = vop(lambda e: e.tensor_sub(WGLB[:, :, :], WGS[:, :, :],
                                     WGHB[:, :, :]), waits=[twg], force=True)
    cready = [call, t_scale, twg]

    wst_free = [None, None]
    wst_ready = [None, None]
    wst_i = [0]

    def wst_load(in_ap, ap_dst=None):
        slot = wst_i[0] % 2
        wst_i[0] += 1
        dst = WST[:, slot, :, :] if ap_dst is None else ap_dst(slot)
        tok = dma(dst, in_ap, "sW0" if slot == 0 else "sW1",
                  waits=[wst_free[slot]])
        wst_ready[slot] = tok
        return slot

    mv_free = [None, None]
    mv_ready = [None, None]
    mv_i = [0]

    def mv_load(dst_ap, in_ap, extra=()):
        slot = mv_i[0] % 2
        mv_i[0] += 1
        tok = dma(dst_ap(slot), in_ap, "sM0" if slot == 0 else "sM1",
                  waits=[mv_free[slot]] + list(extra))
        mv_ready[slot] = tok
        return slot

    xt_state = {"free": None}

    def x_load(buf, t0):
        tok = dma(buf[:, :, :],
                  xh.rearrange("(c p) t -> p c t", p=128)[:, :, t0:t0 + TT],
                  "sX", waits=[xt_state["free"]])
        return tok

    bank_free = [None] * 8
    strip_free = [None]

    def _asl(w, dc, j0, j1):
        return w[dc, :, :].rearrange("p (c j) -> p c j", c=NCH)[:, :, j0:j1]
    w1slice = w1h.rearrange("e (c p) f -> p e c f", p=128)
    w2slice = w2h.rearrange("e (f p) d -> p e f d", p=128)

    pp_state = {"i": 0, "free": [None, None], "bfree": [None, None]}
    pp_slices = [("w1", e_i, fcg) for e_i in range(E) for fcg in range(FC)]
    pp_slices += [("w2", e_i, fcg) for e_i in range(E) for fcg in range(FC)]

    def emit_prepass(nslices):
        for _ in range(nslices):
            i = pp_state["i"]
            if i >= len(pp_slices):
                return
            pp_state["i"] = i + 1
            kind, e_i, fcg = pp_slices[i]
            slot = i % 2
            if kind == "w1":
                src = w1slice[:, e_i, :, fcg * 128:(fcg + 1) * 128]
            else:
                src = w2slice[:, e_i, fcg, :].rearrange(
                    "p (a b) -> p a b", a=NCH)
            tin = dma(PPF[slot][:, :, :], src, "sPP",
                      waits=[pp_state["free"][slot]])
            th = vop(lambda e, s_=slot: e.tensor_copy(
                out=PPB[:, s_, 0, :, :], in_=PPF[s_][:, :, :]),
                waits=[tin, pp_state["bfree"][slot]])
            tl = vop(lambda e, s_=slot: e.tensor_sub(
                PPB[:, s_, 1, :, :], PPF[s_][:, :, :], PPB[:, s_, 0, :, :]),
                waits=[th], force=True)
            pp_state["free"][slot] = tl
            if kind == "w1":
                dsth = w1s[e_i, fcg, :, 0:NCH * 128].rearrange(
                    "p (a b) -> p a b", a=NCH)
                dstl = w1s[e_i, fcg, :, NCH * 128:2 * NCH * 128].rearrange(
                    "p (a b) -> p a b", a=NCH)
                dma(dsth, PPB[:, slot, 0, :, :], "sPP", waits=[tl])
                tout = dma(dstl, PPB[:, slot, 1, :, :], "sPP", waits=[tl])
            else:
                for dh2 in range(2):
                    dma(w2s[e_i, fcg, dh2, :, 0, :],
                        PPB[:, slot, 0, :, :].rearrange(
                            "p b c -> p (b c)")[:, dh2 * 512:(dh2 + 1) * 512],
                        "sPP", waits=[tl])
                    tout = dma(w2s[e_i, fcg, dh2, :, 1, :],
                               PPB[:, slot, 1, :, :].rearrange(
                                   "p b c -> p (b c)")[:, dh2 * 512:
                                                       (dh2 + 1) * 512],
                               "sPP", waits=[tl])
            pp_state["bfree"][slot] = tout

    def layernorm(buf, sqs, strip, src_toks, affine, extra_sq_waits=()):
        tsq = aop(lambda e, b_=buf, q_=sqs: e.activation(
            out=q_[:, :, :], in_=b_[:, :, :], func=ACTF.Square, bias=f32(0.0),
            scale=f32(1.0)), waits=list(src_toks) + list(extra_sq_waits))
        t6 = None
        for c in range(NCH):
            t6 = mm(PS[0:1, 6, 0:TT], ONES[:, 0:1], buf[:, c, :], c == 0,
                    c == NCH - 1, waits=[bank_free[6]] + list(src_toks),
                    inc=(c == NCH - 1))
        t7 = None
        for c in range(NCH):
            t7 = mm(PS[0:1, 7, 0:TT], ONES[:, 0:1], sqs[:, c, :], c == 0,
                    c == NCH - 1, waits=[bank_free[7], tsq],
                    inc=(c == NCH - 1))
        tm = vop(lambda e, s_=strip: e.tensor_scalar_mul(
            s_[0:1, 0, :], PS[0:1, 6, 0:TT], float(1.0 / D)),
            waits=[t6, strip_free[0]])
        bank_free[6] = tm
        tmm = vop(lambda e, s_=strip: e.tensor_mul(
            s_[0:1, 1, :], s_[0:1, 0, :], s_[0:1, 0, :]),
            waits=[tm], force=True)
        tv = vop(lambda e, s_=strip: e.scalar_tensor_tensor(
            out=s_[0:1, 1, :], in0=PS[0:1, 7, 0:TT], scalar=float(1.0 / D),
            in1=s_[0:1, 1, :], op0=ALU.mult, op1=ALU.subtract),
            waits=[t7, tmm], force=True)
        bank_free[7] = tv
        ts = aop(lambda e, s_=strip: e.activation(
            out=s_[0:1, 1, :], in_=s_[0:1, 1, :], func=ACTF.Sqrt,
            bias=f32(LN_EPS), scale=f32(1.0)), waits=[tv])
        tr = vop(lambda e, s_=strip: e.reciprocal(s_[0:1, 1, :],
                                                  s_[0:1, 1, :]),
                 waits=[ts])
        tb4 = mm(PS[:, 4, 0:TT], ONES[0:1, 0:128], strip[0:1, 0, :], True,
                 True, waits=[bank_free[4], tm], inc=True)
        tb5 = mm(PS[:, 5, 0:TT], ONES[0:1, 0:128], strip[0:1, 1, :], True,
                 True, waits=[bank_free[5], tr], inc=True)
        strip_free[0] = tb5
        last = None
        for c in range(NCH):
            vop(lambda e, c_=c, b_=buf: e.tensor_sub(
                b_[:, c_, :], b_[:, c_, :], PS[:, 4, 0:TT]), waits=[tb4])
            last = vop(lambda e, c_=c, b_=buf: e.tensor_mul(
                b_[:, c_, :], b_[:, c_, :], PS[:, 5, 0:TT]), waits=[tb5])
            if affine:
                last = vop(lambda e, c_=c, b_=buf: e.tensor_scalar(
                    out=b_[:, c_, :], in0=b_[:, c_, :],
                    scalar1=GAM[:, c_:c_ + 1], scalar2=BET[:, c_:c_ + 1],
                    op0=ALU.mult, op1=ALU.add))
        bank_free[4] = last
        bank_free[5] = last
        return last

    mix_free = None
    ht_free = None
    yb_free = None
    vaug_free = None
    breg_free = []
    for s in range(2):
        set_guard()
        kvz = vop(lambda e: e.memset(KV[:, :, :, :], 0.0),
                  waits=(cready if s == 0 else ()))
        vones = vop(lambda e: e.memset(P1_VAUG[:, :, :, DH:DH + 1], 1.0))
        for ti in range(NTS):
            t0 = s * SEQ + ti * TT
            if s == 0:
                emit_prepass(32)
            xw = x_load(P1_XT, t0)
            kv_last_mm = None
            for half in range(2):
                vlast = None
                for oc in range(4):
                    oct_ = half * 4 + oc
                    ms = mv_load(lambda sl: P1_WMV[:, sl, :, :],
                                 _asl(wv, oct_, 0, 128),
                                 extra=breg_free if oc == 0 else ())
                    for tt in range(NSUB):
                        b = (oc * NSUB + tt) % 2
                        mm(PS[:, b, 0:128], ONES[0:1, 0:128],
                           BVS[0:1, 0, oct_ * 128:(oct_ + 1) * 128], True,
                           False,
                           waits=[bank_free[b], mv_ready[ms], xw] + cready)
                        tok = None
                        for c in range(NCH):
                            tok = mm(PS[:, b, 0:128],
                                     P1_XT[:, c, tt * 128:(tt + 1) * 128],
                                     P1_WMV[:, ms, c, :], False, c == NCH - 1,
                                     inc=(c == NCH - 1))
                        if tt == NSUB - 1:
                            mv_free[ms] = tok
                        vlast = vop(lambda e, b_=b, tt_=tt, q_=oc:
                                    e.tensor_copy(
                                        out=P1_VAUG[:, tt_,
                                                    q_ * 2:q_ * 2 + 2, 0:DH],
                                        in_=PS[:, b_, 0:128].rearrange(
                                            "p (a b) -> p a b", a=2)),
                                    waits=[tok, vaug_free])
                        bank_free[b] = vlast
                vdone = vlast
                evtoks = []
                for cl in range(4):
                    cglob = half * 4 + cl
                    ws = wst_load(_asl(wk, cglob, 0, 128))
                    b = 2 + (cl % 2)
                    tok = None
                    for c in range(NCH):
                        tok = mm(PS[:, b, 0:TT], WST[:, ws, c, :],
                                 P1_XT[:, c, :], c == 0, c == NCH - 1,
                                 waits=([bank_free[b], wst_ready[ws], xw,
                                         mv_free[0], mv_free[1]]
                                        if c == 0 else ()),
                                 inc=(c == NCH - 1))
                    wst_free[ws] = tok
                    ev = aop(lambda e, b_=b, cl_=cl, cg=cglob:
                             e.activation(out=P1_KQ[:, cl_, :],
                                          in_=PS[:, b_, 0:TT],
                                          func=ACTF.Identity,
                                          bias=BK[:, cg:cg + 1],
                                          scale=f32(1.0)),
                             waits=[tok])
                    evtoks.append(ev)
                    bank_free[b] = ev
                sq2_free = None
                tsq = None
                for cl in range(4):
                    cglob = half * 4 + cl
                    tsq = aop(lambda e, cl_=cl:
                              e.activation(out=P1_SQ2[:, :],
                                           in_=P1_KQ[:, cl_, :],
                                           func=ACTF.Square, bias=f32(0.0),
                                           scale=f32(1.0)),
                              waits=[evtoks[cl], sq2_free])
                    for dlt in range(2):
                        h = 2 * cglob + dlt
                        hloc = h - 8 * half
                        base = dlt * 64
                        eps_toks = []
                        for tt in range(NSUB):
                            b = tt % 2
                            mm(PS[:, b, 0:NF],
                               P1_KQ[base:base + DH, cl,
                                     tt * 128:(tt + 1) * 128],
                               CPD[base:base + DH, :], True, False,
                               waits=[bank_free[b], evtoks[cl]])
                            tok = mm(PS[:, b, 0:NF],
                                     P1_SQ2[base:base + DH,
                                            tt * 128:(tt + 1) * 128],
                                     NEG1[base:base + DH, 0:NF], False, True,
                                     waits=[tsq], inc=True)
                            sq2_free = tok
                            te = aop(lambda e, b_=b, tt_=tt:
                                     e.activation(out=P1_KP[:, tt_, :],
                                                  in_=PS[:, b_, 0:NF],
                                                  func=ACTF.Exp,
                                                  bias=f32(LNRATIO),
                                                  scale=f32(1.0)),
                                     waits=[tok])
                            bank_free[b] = te
                            tp = vop(lambda e, tt_=tt:
                                     e.tensor_scalar_add(P1_KP[:, tt_, :],
                                                         P1_KP[:, tt_, :],
                                                         float(EPSR)),
                                     waits=[te])
                            eps_toks.append(tp)
                        for fc2 in range(2):
                            b = 4 + fc2
                            tok = None
                            for tt in range(NSUB):
                                tok = mm(PS[:, b, 0:DH + 1],
                                         P1_KP[:, tt,
                                               fc2 * 128:(fc2 + 1) * 128],
                                         P1_VAUG[:, tt, hloc, :], tt == 0,
                                         tt == NSUB - 1,
                                         waits=([bank_free[b], eps_toks[tt],
                                                 vdone, kvz, vones]
                                                if tt == 0 else
                                                [eps_toks[tt]]),
                                         inc=(tt == NSUB - 1))
                            kv_last_mm = tok
                            tkv = vop(lambda e, b_=b, f_=fc2, h_=h:
                                      e.tensor_add(KV[:, f_, h_, :],
                                                   KV[:, f_, h_, :],
                                                   PS[:, b_, 0:DH + 1]),
                                      waits=[tok])
                            bank_free[b] = tkv
                vaug_free = kv_last_mm
                breg_free = [kv_last_mm, tsq]
            xt_state["free"] = kv_last_mm
        kv_done_dv = ("sDV", p.tot["sDV"])
        set_guard()
        for ti in range(NTS):
            t0 = s * SEQ + ti * TT
            if s == 0:
                emit_prepass(32)
            xw = x_load(P2_XT, t0)
            zin_last = None
            sq2_free = None
            kq_free = None
            for half in range(2):
                evtoks = []
                for cl in range(4):
                    cglob = half * 4 + cl
                    ws = wst_load(_asl(wq, cglob, 0, 128))
                    b = 2 + (cl % 2)
                    tok = None
                    for c in range(NCH):
                        tok = mm(PS[:, b, 0:TT], WST[:, ws, c, :],
                                 P2_XT[:, c, :], c == 0, c == NCH - 1,
                                 waits=([bank_free[b], wst_ready[ws], xw]
                                        if c == 0 else ()),
                                 inc=(c == NCH - 1))
                    wst_free[ws] = tok
                    ev = aop(lambda e, b_=b, cl_=cl, cg=cglob:
                             e.activation(out=P2_KQ[:, cl_, :],
                                          in_=PS[:, b_, 0:TT],
                                          func=ACTF.Identity,
                                          bias=BQ[:, cg:cg + 1],
                                          scale=f32(1.0)),
                             waits=[tok, kq_free])
                    evtoks.append(ev)
                    bank_free[b] = ev
                for cl in range(4):
                    cglob = half * 4 + cl
                    tsq = aop(lambda e, cl_=cl:
                              e.activation(out=P2_SQ2[:, :],
                                           in_=P2_KQ[:, cl_, :],
                                           func=ACTF.Square, bias=f32(0.0),
                                           scale=f32(1.0)),
                              waits=[evtoks[cl], sq2_free])
                    for dlt in range(2):
                        h = 2 * cglob + dlt
                        base = dlt * 64
                        eps_toks = []
                        for fc2 in range(2):
                            b = fc2
                            mm(PS[:, b, 0:TT],
                               CPD[base:base + DH, fc2 * 128:(fc2 + 1) * 128],
                               P2_KQ[base:base + DH, cl, :], True, False,
                               waits=[bank_free[b], evtoks[cl]])
                            tok = mm(PS[:, b, 0:TT],
                                     NEG1[base:base + DH, 0:128],
                                     P2_SQ2[base:base + DH, :], False, True,
                                     waits=[tsq], inc=True)
                            sq2_free = tok
                            te = aop(lambda e, b_=b, f_=fc2:
                                     e.activation(out=P2_QP[:, f_, :],
                                                  in_=PS[:, b_, 0:TT],
                                                  func=ACTF.Exp,
                                                  bias=f32(LNRATIO),
                                                  scale=f32(1.0)),
                                     waits=[tok])
                            bank_free[b] = te
                            tp = vop(lambda e, f_=fc2:
                                     e.tensor_scalar_add(P2_QP[:, f_, :],
                                                         P2_QP[:, f_, :],
                                                         float(EPSR)),
                                     waits=[te])
                            eps_toks.append(tp)
                        ba = 4 + (h % 2)
                        bn = 6 + (h % 2)
                        tok = None
                        for fc2 in range(2):
                            tok = mm(PS[base:base + DH, ba, 0:TT],
                                     KV[:, fc2, h, 0:DH],
                                     P2_QP[:, fc2, :], fc2 == 0, fc2 == 1,
                                     waits=([bank_free[ba], eps_toks[0],
                                             eps_toks[1], kv_done_dv]
                                            if fc2 == 0 else ()),
                                     inc=(fc2 == 1))
                        qkv_mm = tok
                        tok = None
                        for fc2 in range(2):
                            tok = mm(PS[0:1, bn, 0:TT],
                                     KV[:, fc2, h, DH:DH + 1],
                                     P2_QP[:, fc2, :], fc2 == 0, fc2 == 1,
                                     waits=([bank_free[bn]] if fc2 == 0
                                            else ()),
                                     inc=(fc2 == 1))
                        nrm_mm = tok
                        tn = vop(lambda e, b_=bn:
                                 e.tensor_scalar_add(P2_STRIP[0:1, 0, :],
                                                     PS[0:1, b_, 0:TT],
                                                     float(1e-6)),
                                 waits=[nrm_mm, strip_free[0]])
                        tn = vop(lambda e: e.reciprocal(P2_STRIP[0:1, 1, :],
                                                        P2_STRIP[0:1, 0, :]),
                                 waits=[tn], force=True)
                        tb = mm(PS[:, bn, 0:TT], ONES[0:1, 0:128],
                                P2_STRIP[0:1, 1, :], True, True,
                                waits=[tn], inc=True)
                        strip_free[0] = tb
                        vop(lambda e, b_=ba, bs=base, c2=cglob:
                            e.tensor_copy(out=P2_ZIN[bs:bs + DH, c2, :],
                                          in_=PS[bs:bs + DH, b_, 0:TT]),
                            waits=[qkv_mm])
                        zin_last = vop(
                            lambda e, bn_=bn, bs=base, c2=cglob:
                            e.tensor_mul(P2_ZIN[bs:bs + DH, c2, :],
                                         P2_ZIN[bs:bs + DH, c2, :],
                                         PS[bs:bs + DH, bn_, 0:TT]),
                            waits=[tb])
                        bank_free[ba] = zin_last
                        bank_free[bn] = zin_last
                    kq_free = ("sPE", p.tot["sPE"])
            ev_toks = []
            wo_mm_last = None
            for dc in range(NCH):
                ws = wst_load(_asl(wo, dc, 0, 128))
                b = 2 + (dc % 2)
                tok = None
                for c in range(NCH):
                    tok = mm(PS[:, b, 0:TT], WST[:, ws, c, :],
                             P2_ZIN[:, c, :], c == 0, c == NCH - 1,
                             waits=([bank_free[b], wst_ready[ws], zin_last]
                                    if c == 0 else ()),
                             inc=(c == NCH - 1))
                wst_free[ws] = tok
                wo_mm_last = tok
                ev = vop(lambda e, b_=b, dc_=dc:
                         e.scalar_tensor_tensor(
                             out=P2_XT[:, dc_, :], in0=PS[:, b_, 0:TT],
                             scalar=BO[:, dc_:dc_ + 1],
                             in1=P2_XT[:, dc_, :],
                             op0=ALU.add, op1=ALU.add),
                         waits=[tok])
                ev_toks.append(ev)
                bank_free[b] = ev
            layernorm(P2_XT, P2_SQS, P2_STRIP, ev_toks, affine=False,
                      extra_sq_waits=[wo_mm_last])
            cplast = None
            t0l = ti * TT
            for c in range(NCH):
                cplast = vop(lambda e, c_=c, t0_=t0l: e.tensor_copy(
                    out=REG[:, c_, t0_:t0_ + TT], in_=P2_XT[:, c_, :]))
            xt_state["free"] = cplast
            if debug:
                dma(dbg["y1"].rearrange("(c p) t -> p c t", p=128)
                    [:, :, t0:t0 + TT], REG[:, :, t0l:t0l + TT], "sOut",
                    waits=[cplast])
        ppdone = ("sPP", p.tot["sPP"])
        for ti in range(NTS):
            set_guard()
            if ti == 0:
                guard.append(ppdone)
            t0l = ti * TT
            t0 = s * SEQ + t0l
            regslot = REG[:, :, t0l:t0l + TT]
            ylast = None
            for c in range(NCH):
                th = vop(lambda e, c_=c, r_=regslot: e.tensor_copy(
                    out=YHB[:, c_, :], in_=r_[:, c_, :]), waits=[yb_free])
                ylast = vop(lambda e, c_=c, r_=regslot: e.tensor_sub(
                    YLB[:, c_, :], r_[:, c_, :], YHB[:, c_, :]),
                    waits=[th], force=True)
            wtoks = []
            for tt in range(NSUB):
                b = 6 + (tt % 2)
                mm(PS[:, b, 0:E], ONES[0:1, 0:128], BGS[0:1, :], True, False,
                   waits=[bank_free[b], ylast])
                tok = None
                for c in range(NCH):
                    sl = slice(tt * 128, (tt + 1) * 128)
                    mm(PS[:, b, 0:E], YHB[:, c, sl], WGHB[:, c, :], False, False)
                    mm(PS[:, b, 0:E], YHB[:, c, sl], WGLB[:, c, :], False, False)
                    tok = mm(PS[:, b, 0:E], YLB[:, c, sl], WGHB[:, c, :], False,
                             c == NCH - 1, inc=(c == NCH - 1))
                t1 = vop(lambda e, b_=b: e.reduce_max(GS[:, 24:25],
                                                      PS[:, b_, 0:E], axis=AX.X),
                         waits=[tok])
                teq = vop(lambda e, b_=b: e.tensor_scalar(
                    out=GS[:, 0:8], in0=PS[:, b_, 0:E], scalar1=GS[:, 24:25],
                    scalar2=None, op0=ALU.is_equal), waits=[t1], force=True)
                tst = vop(lambda e, b_=b: e.scalar_tensor_tensor(
                    out=GS[:, 0:8], in0=GS[:, 0:8], scalar=float(-1e30),
                    in1=PS[:, b_, 0:E], op0=ALU.mult, op1=ALU.add),
                    waits=[teq], force=True)
                tm2 = vop(lambda e: e.reduce_max(GS[:, 26:27], GS[:, 0:8],
                                                 axis=AX.X), waits=[tst],
                          force=True)
                tneg = vop(lambda e: e.tensor_scalar_mul(GS[:, 25:26],
                                                         GS[:, 24:25],
                                                         float(-1.0)))
                tgem = vop(lambda e, b_=b: e.tensor_scalar(
                    out=GS[:, 8:16], in0=PS[:, b_, 0:E], scalar1=GS[:, 26:27],
                    scalar2=None, op0=ALU.is_ge), waits=[tm2], force=True)
                tel = aop(lambda e, b_=b: e.activation(
                    out=GS[:, 16:24], in_=PS[:, b_, 0:E], func=ACTF.Exp,
                    bias=GS[:, 25:26], scale=f32(1.0)), waits=[tneg, tgem])
                tdn = aop(lambda e: e.activation(
                    out=GS[:, 27:28], in_=GS[:, 26:27], func=ACTF.Exp,
                    bias=GS[:, 25:26], scale=f32(1.0)))
                bank_free[b] = [tgem, tel]
                tda = vop(lambda e: e.tensor_scalar_add(GS[:, 27:28],
                                                        GS[:, 27:28],
                                                        float(1.0)), waits=[tdn])
                trdn = vop(lambda e: e.reciprocal(GS[:, 28:29], GS[:, 27:28]),
                           waits=[tda], force=True)
                vop(lambda e, tt_=tt: e.tensor_mul(WTM[:, tt_, :], GS[:, 16:24],
                                                   GS[:, 8:16]),
                    waits=[tel, tgem], force=True)
                tw = vop(lambda e, tt_=tt: e.tensor_scalar_mul(
                    WTM[:, tt_, :], WTM[:, tt_, :], GS[:, 28:29]),
                    waits=[trdn], force=True)
                wtoks.append(tw)
            trtoks = []
            for tt in range(NSUB):
                tok = p.op("tensor",
                           lambda e, tt_=tt: e.transpose(
                               PS[0:E, 7, tt_ * 128:(tt_ + 1) * 128],
                               WTM[:, tt_, :], EYE[:, :]),
                           waits=[wtoks[tt], bank_free[7]] + guard,
                           incs=(("sPE", 1),))
                trtoks.append(tok)
            tw8 = vop(lambda e: e.tensor_copy(out=WFM8[:, :],
                                              in_=PS[0:E, 7, 0:TT]),
                      waits=trtoks)
            bank_free[7] = tw8
            eo_ev_last = None
            for e_i in range(E):
                twb = mm(PS[:, 6, 0:TT], SEL8[:, e_i * 128:(e_i + 1) * 128],
                         WFM8[:, :], True, True,
                         waits=[tw8, bank_free[6]], inc=True)
                twbc = vop(lambda e: e.tensor_copy(out=F_WBC[:, :],
                                                   in_=PS[:, 6, 0:TT]),
                           waits=[twb])
                bank_free[6] = twbc
                lim_toks = []
                for fcg in range(FC):
                    ws = wst_load(w1s[e_i, fcg, :, :].rearrange('p (l c j) -> p l c j', l=2, c=NCH),
                                  ap_dst=lambda sl: WSB[:, sl, :, :, :])
                    b = fcg % 2
                    tok = None
                    for c in range(NCH):
                        mm(PS[:, b, 0:TT], WSB[:, ws, 0, c, :], YHB[:, c, :],
                           c == 0, False,
                           waits=([bank_free[b], wst_ready[ws], ylast]
                                  if c == 0 else ()))
                        mm(PS[:, b, 0:TT], WSB[:, ws, 0, c, :], YLB[:, c, :],
                           False, False)
                        tok = mm(PS[:, b, 0:TT], WSB[:, ws, 1, c, :],
                                 YHB[:, c, :], False, c == NCH - 1,
                                 inc=(c == NCH - 1))
                    wst_free[ws] = tok
                    trl = aop(lambda e, b_=b, fc_=fcg, e__=e_i:
                              e.activation(out=F_RELU[:, :], in_=PS[:, b_, 0:TT],
                                           func=ACTF.Relu,
                                           bias=B1S[:, e__, fc_:fc_ + 1],
                                           scale=f32(1.0)),
                              waits=[tok, ht_free,
                                     lim_toks[-1] if lim_toks else None])
                    bank_free[b] = trl
                    tmu = vop(lambda e: e.tensor_mul(
                        F_RELU[:, :], F_RELU[:, :], F_WBC[:, :]),
                        waits=[trl, twbc])
                    thh = vop(lambda e, fc_=fcg: e.tensor_copy(
                        out=HTH[:, fc_, :], in_=F_RELU[:, :]),
                        waits=[tmu], force=True)
                    tll = vop(lambda e, fc_=fcg: e.tensor_sub(
                        HTL[:, fc_, :], F_RELU[:, :], HTH[:, fc_, :]),
                        waits=[thh], force=True)
                    lim_toks.append(tll)
                for dh2 in range(2):
                    gtoks = [None] * 4
                    if e_i == 0:
                        for dl in range(4):
                            dc = dh2 * 4 + dl
                            gtoks[dl] = mm(PS[:, 2 + dl, 0:TT],
                                           B2S[:, dc * 128:(dc + 1) * 128],
                                           WFM8[:, :], True, False,
                                           waits=[bank_free[2 + dl], tw8])
                    for fcg in range(FC):
                        wm = mv_load(lambda sl: W2B[:, sl, :, :],
                                     w2s[e_i, fcg, dh2, :, :, :])
                        for dl in range(4):
                            st = (fcg == 0) and (e_i != 0)
                            sp = (fcg == FC - 1)
                            w = [mv_ready[wm], lim_toks[fcg]]
                            if fcg == 0 and e_i != 0:
                                w.append(bank_free[2 + dl])
                            sl = slice(dl * 128, (dl + 1) * 128)
                            mm(PS[:, 2 + dl, 0:TT], W2B[:, wm, 0, sl],
                               HTH[:, fcg, :], st, False, waits=w)
                            mm(PS[:, 2 + dl, 0:TT], W2B[:, wm, 0, sl],
                               HTL[:, fcg, :], False, False)
                            gtoks[dl] = mm(PS[:, 2 + dl, 0:TT],
                                           W2B[:, wm, 1, sl],
                                           HTH[:, fcg, :], False, sp,
                                           inc=(sp or dl == 3))
                        mv_free[wm] = gtoks[3]
                    for dl in range(4):
                        dc = dh2 * 4 + dl
                        if e_i == 0:
                            ev = vop(lambda e, dl_=dl, dc_=dc: e.tensor_copy(
                                out=F_MIX[:, dc_, :], in_=PS[:, 2 + dl_, 0:TT]),
                                waits=[gtoks[dl], mix_free])
                        else:
                            ev = vop(lambda e, dl_=dl, dc_=dc: e.tensor_add(
                                F_MIX[:, dc_, :], F_MIX[:, dc_, :],
                                PS[:, 2 + dl_, 0:TT]),
                                waits=[gtoks[dl]])
                        bank_free[2 + dl] = ev
                        eo_ev_last = ev
                ht_free = ("sPE", p.tot["sPE"])
            yb_free = ("sPE", p.tot["sPE"])
            add_toks = []
            for c in range(NCH):
                add_toks.append(vop(lambda e, c_=c, r_=regslot: e.tensor_add(
                    F_MIX[:, c_, :], F_MIX[:, c_, :], r_[:, c_, :]),
                    waits=[eo_ev_last]))
            layernorm(F_MIX, F_SQS, F_STRIP, [add_toks[-1]], affine=True,
                      extra_sq_waits=[ht_free])
            add_toks = []
            for c in range(NCH):
                add_toks.append(vop(lambda e, c_=c, r_=regslot: e.tensor_add(
                    F_MIX[:, c_, :], F_MIX[:, c_, :], r_[:, c_, :])))
            lnlast = layernorm(F_MIX, F_SQS, F_STRIP, [add_toks[-1]],
                               affine=False)
            if debug:
                dma(dbg["mix"].rearrange("(c p) t -> p c t", p=128)
                    [:, :, t0:t0 + TT], F_MIX[:, :, :], "sOut", waits=[lnlast])
            dma(outh.rearrange("(c p) t -> p c t", p=128)[:, :, t0:t0 + TT],
                F_MIX[:, :, :], "sOut", waits=[lnlast])
            mix_free = ("sOut", p.tot["sOut"])



    p.op("sync", None, waits=[("sOut", p.tot["sOut"])])

    import contextlib
    es = contextlib.ExitStack()
    sems = {s: es.enter_context(nc.semaphore(s)) for s in p.tot}
    block = es.enter_context(nc.Block())

    def emit(eng_name):
        def body(eng):
            for fn, waits, incs in p.items[eng_name]:
                for s, v in waits:
                    eng.wait_ge(sems[s], v)
                if fn is None:
                    continue
                instr = fn(eng)
                for s, d in incs:
                    instr.then_inc(sems[s], d)
        return body

    block.sync(emit("sync"))
    block.tensor(emit("tensor"))
    block.vector(emit("vector"))
    block.scalar(emit("scalar"))
    block.gpsimd(emit("gpsimd"))
    es.close()
    return nc


def _wre(w):
    w = np.asarray(w, np.float32).reshape(NCH, 128, NCH, 128)
    return np.ascontiguousarray(w.transpose(2, 1, 0, 3).reshape(NCH, 128, D))


def make_inmaps(inputs):
    q = np.asarray(inputs["L0_attn_queries"], np.float32)
    base = {
        "wq": _wre(inputs["L0_attn_Wq"]),
        "wk": _wre(inputs["L0_attn_Wk"]),
        "wv": _wre(inputs["L0_attn_Wv"]),
        "wo": _wre(inputs["L0_attn_Wo"]),
        "bq": np.asarray(inputs["L0_attn_bq"], np.float32).reshape(1, D),
        "bk": np.asarray(inputs["L0_attn_bk"], np.float32).reshape(1, D),
        "bv": np.asarray(inputs["L0_attn_bv"], np.float32).reshape(1, D),
        "bo": np.asarray(inputs["L0_attn_bo"], np.float32).reshape(1, D),
        "proj": np.ascontiguousarray(inputs["L0_attn_proj"], np.float32),
        "wg": np.ascontiguousarray(inputs["L0_ffn_Wg"], np.float32),
        "bg": np.asarray(inputs["L0_ffn_bg"], np.float32).reshape(1, E),
        "w1": np.ascontiguousarray(inputs["L0_ffn_W1"], np.float32),
        "b1": np.ascontiguousarray(inputs["L0_ffn_b1"], np.float32),
        "w2": np.ascontiguousarray(inputs["L0_ffn_W2"], np.float32),
        "b2": np.ascontiguousarray(inputs["L0_ffn_b2"], np.float32),
        "gam": np.asarray(inputs["L0_ffn_gamma"], np.float32).reshape(1, D),
        "bet": np.asarray(inputs["L0_ffn_beta"], np.float32).reshape(1, D),
        "eye": np.eye(128, dtype=np.float32),
        "sel8": np.kron(np.eye(E, dtype=np.float32),
                        np.ones((1, 128), np.float32)),
    }
    in_maps = []
    for c in range(8):
        m = dict(base)
        m["x"] = np.ascontiguousarray(q[2 * c:2 * c + 2].reshape(T, D).T)
        in_maps.append(m)
    return in_maps


def kernel(**inputs):
    in_maps = make_inmaps(inputs)
    nc = build_program()
    res = run_bass_kernel_spmd(nc, in_maps, core_ids=list(range(8)))
    outs = [np.asarray(r["out"]) for r in res.results]
    full = np.stack([o.T.reshape(2, SEQ, D) for o in outs])
    return np.ascontiguousarray(full.reshape(16, SEQ, D), dtype=np.float32)


# version 10 (final)
# speedup vs baseline: 1.3120x; 1.0322x over previous
import numpy as np
import concourse.bass as bass
import concourse.mybir as mybir
from concourse.bass_utils import run_bass_kernel_spmd

F32 = mybir.dt.float32
BF16 = mybir.dt.bfloat16
ALU = mybir.AluOpType
ACTF = mybir.ActivationFunctionType
AX = mybir.AxisListType
f32 = np.float32

D = 1024
NCH = 8
H = 16
DH = 64
NF = 256
E = 8
FF = 2048
FC = 16
SEQ = 2048
T = 4096
TT = 512
NTS = SEQ // TT
NT = T // TT
NSUB = TT // 128

C1 = 64.0 ** -0.25
NEGH = -(C1 * C1) / 2.0
LNRATIO = float(np.log(1.0 / np.sqrt(NF)))
EPSR = 1e-6 / 16.0
LN_EPS = 1e-5

ENGINES = ("sync", "tensor", "vector", "scalar", "gpsimd")


class Prog:
    def __init__(self):
        self.items = {e: [] for e in ENGINES}
        self.tot = {}
        self.self_contrib = {e: {} for e in ENGINES}
        self.last_wait = {e: {} for e in ENGINES}

    def sem(self, name):
        self.tot.setdefault(name, 0)

    def op(self, eng, fn, waits=(), incs=(), force=False):
        w = []
        flat = []
        for tok in waits:
            if tok is None:
                continue
            if isinstance(tok, list):
                flat.extend(t for t in tok if t is not None)
            else:
                flat.append(tok)
        for tok in flat:
            s, v = tok
            assert v <= self.tot[s], (s, v, self.tot[s])
            if v <= self.last_wait[eng].get(s, 0):
                continue
            if not force and v <= self.self_contrib[eng].get(s, 0):
                continue
            self.last_wait[eng][s] = v
            w.append((s, v))
        out = None
        for s, d in incs:
            self.tot[s] += d
            self.self_contrib[eng][s] = self.tot[s]
            out = (s, self.tot[s])
        self.items[eng].append((fn, tuple(w), tuple(incs)))
        return out


def build_program(debug=False):
    nc = bass.Bass("TRN2", target_bir_lowering=False, debug=False,
                   detect_race_conditions=False, dynamic_dma_scratch_size=2048)
    p = Prog()

    xh = nc.dram_tensor("x", [D, T], F32, kind="ExternalInput").ap()
    wq = nc.dram_tensor("wq", [NCH, 128, D], F32, kind="ExternalInput").ap()
    wk = nc.dram_tensor("wk", [NCH, 128, D], F32, kind="ExternalInput").ap()
    wv = nc.dram_tensor("wv", [NCH, 128, D], F32, kind="ExternalInput").ap()
    wo = nc.dram_tensor("wo", [NCH, 128, D], F32, kind="ExternalInput").ap()
    bqh = nc.dram_tensor("bq", [1, D], F32, kind="ExternalInput").ap()
    bkh = nc.dram_tensor("bk", [1, D], F32, kind="ExternalInput").ap()
    bvh = nc.dram_tensor("bv", [1, D], F32, kind="ExternalInput").ap()
    boh = nc.dram_tensor("bo", [1, D], F32, kind="ExternalInput").ap()
    projh = nc.dram_tensor("proj", [DH, NF], F32, kind="ExternalInput").ap()
    wgh = nc.dram_tensor("wg", [D, E], F32, kind="ExternalInput").ap()
    bgh = nc.dram_tensor("bg", [1, E], F32, kind="ExternalInput").ap()
    w1h = nc.dram_tensor("w1", [E, D, FF], F32, kind="ExternalInput").ap()
    b1h = nc.dram_tensor("b1", [E, FF], F32, kind="ExternalInput").ap()
    w2h = nc.dram_tensor("w2", [E, FF, D], F32, kind="ExternalInput").ap()
    b2h = nc.dram_tensor("b2", [E, D], F32, kind="ExternalInput").ap()
    gamh = nc.dram_tensor("gam", [1, D], F32, kind="ExternalInput").ap()
    beth = nc.dram_tensor("bet", [1, D], F32, kind="ExternalInput").ap()
    eyeh = nc.dram_tensor("eye", [128, 128], F32, kind="ExternalInput").ap()
    sel8h = nc.dram_tensor("sel8", [E, E * 128], F32, kind="ExternalInput").ap()
    outh = nc.dram_tensor("out", [D, T], F32, kind="ExternalOutput").ap()
    w1s = nc.dram_tensor("w1scr", [E, FC, 128, 2 * NCH * 128], BF16,
                         kind="Internal").ap()
    w2s = nc.dram_tensor("w2scr", [E, FC, 2, 128, 2, 512], BF16,
                         kind="Internal").ap()
    dbg = {}
    if debug:
        dbg["y1"] = nc.dram_tensor("dbg_y1", [D, T], F32,
                                   kind="ExternalOutput").ap()
        dbg["mix"] = nc.dram_tensor("dbg_mix", [D, T], F32,
                                    kind="ExternalOutput").ap()

    def sb(name, shape, dtype=F32):
        return nc.alloc_sbuf_tensor(name, shape, dtype).ap()

    REG = sb("reg", [128, NCH, SEQ])
    KV = sb("kvsb", [128, 2, H, DH + 1])
    WST = sb("wst", [128, 2, NCH, 128])
    CPD = sb("cpd", [128, NF])
    NEG1 = sb("neg1", [128, NF])
    ONES = sb("ones", [128, 128])
    EYE = sb("eye_s", [128, 128])
    SEL8 = sb("sel8_s", [E, E * 128])
    B2S = sb("b2s", [E, D])
    BQ = sb("bq_s", [128, NCH])
    BK = sb("bk_s", [128, NCH])
    BO = sb("bo_s", [128, NCH])
    GAM = sb("gam_s", [128, NCH])
    BET = sb("bet_s", [128, NCH])
    B1S = sb("b1s", [128, E, FC])
    WGS = sb("wgs", [128, NCH, E])
    WGHB = sb("wghb", [128, NCH, E], BF16)
    WGLB = sb("wglb", [128, NCH, E], BF16)
    BGS = sb("bgs", [1, E])
    WTM = sb("wtm", [128, NSUB, E])
    WFM8 = sb("wfm8", [E, TT])
    GS = sb("gs", [128, 40])

    SCRB_W = 30720
    SCRB = sb("scrb", [128, SCRB_W], BF16)
    AR_W = nc.sbuf_bytes_remaining // 4 - 16
    SCR = sb("scr", [128, AR_W])

    def view_t(base_ap, width, off, shape):
        n = int(np.prod(shape))
        assert off + n <= width, (off, shape, width)
        a = base_ap[:, off:off + n]
        if len(shape) == 1:
            return a
        if len(shape) == 2:
            return a.rearrange("p (a b) -> p a b", a=shape[0])
        if len(shape) == 3:
            return a.rearrange("p (a b c) -> p a b c", a=shape[0], b=shape[1])
        return a.rearrange("p (a b c d) -> p a b c d", a=shape[0],
                           b=shape[1], c=shape[2])

    def view(off, shape):
        return view_t(SCR, AR_W, off, shape)

    def viewb(off, shape):
        return view_t(SCRB, SCRB_W, off, shape)

    o = [0]

    def take(n):
        r = o[0]
        o[0] += n
        return r

    o[0] = 0
    P1_XT = view(take(NCH * TT), (NCH, TT))
    _b = take(2 * NCH * 128)
    P1_WMV = view(_b, (2, NCH, 128))
    P1_KQ = view(_b, (4, TT))
    P1_VAUG = view(take(NSUB * 8 * (DH + 1)), (NSUB, 8, DH + 1))
    P1_SQ2 = view(take(TT), (TT,))
    P1_KP = view(take(NSUB * NF), (NSUB, NF))
    p1_end = o[0]
    o[0] = 0
    P2_XT = view(take(NCH * TT), (NCH, TT))
    P2_KQ = view(take(4 * TT), (4, TT))
    P2_ZIN = view(take(NCH * TT), (NCH, TT))
    P2_SQ2 = view(take(TT), (TT,))
    P2_QP = view(take(2 * TT), (2, TT))
    P2_STRIP = view(take(2 * TT), (2, TT))
    P2_SQS = P2_ZIN
    p2_end = o[0]
    o[0] = 0
    F_MIX = view(take(NCH * TT), (NCH, TT))
    F_SQS = view(take(NCH * TT), (NCH, TT))
    F_WBC = view(take(TT), (TT,))
    F_RELU = view(take(TT), (TT,))
    F_STRIP = view(take(2 * TT), (2, TT))
    f_end = o[0]
    _aux = max(p1_end, p2_end, f_end)
    PPF = [view(_aux, (NCH, 128)), view(_aux + 1024, (NCH, 128))]
    _bvs = _aux + 2048
    BVS = SCR[0:1, _bvs:_bvs + D].rearrange("p (o2 d) -> p o2 d", o2=1)
    aux_end = _bvs + D
    assert aux_end <= AR_W, (aux_end, AR_W)
    o[0] = 0
    YHB = viewb(take(NCH * TT), (NCH, TT))
    YLB = viewb(take(NCH * TT), (NCH, TT))
    HTH = viewb(take(FC * TT), (FC, TT))
    HTL = viewb(take(FC * TT), (FC, TT))
    WSB = viewb(take(2 * 2 * NCH * 128), (2, 2, NCH, 128))
    W2B = viewb(take(2 * 2 * 512), (2, 2, 512))
    fb_end = o[0]
    assert fb_end <= SCRB_W, (fb_end, SCRB_W)
    PPB = viewb(0, (2, 2, NCH, 128))

    PS = nc.alloc_psum_tensor("ps", [128, 8, 512], F32).ap()

    for s in ("sC", "sX", "sW0", "sW1", "sM0", "sM1", "sPE", "sDV", "sAC",
              "sOut", "sPP"):
        p.sem(s)

    guard = []

    def dma(out_ap, in_ap, sem, waits=(), slow=False, q="sync"):
        return p.op(q,
                    lambda e, o_=out_ap, i_=in_ap, s_=slow: e.dma_start(
                        out=o_, in_=i_, allow_slow_non_contiguous=s_),
                    waits=list(waits) + guard, incs=((sem, 16),))

    def mm(out_ap, lhsT, rhs, start, stop, waits=(), inc=False):
        return p.op("tensor",
                    lambda e, o_=out_ap, l_=lhsT, r_=rhs, s0=start, s1=stop:
                    e.matmul(o_, l_, r_, start=s0, stop=s1),
                    waits=list(waits) + guard,
                    incs=((("sPE", 1),) if inc else ()))

    def vop(fn, waits=(), force=False):
        return p.op("vector", fn, waits=list(waits) + guard,
                    incs=(("sDV", 1),), force=force)

    def aop(fn, waits=(), force=False):
        return p.op("scalar", fn, waits=list(waits) + guard,
                    incs=(("sAC", 1),), force=force)

    def set_guard():
        guard.clear()
        guard.extend([("sPE", p.tot["sPE"]), ("sDV", p.tot["sDV"]),
                      ("sAC", p.tot["sAC"]), ("sOut", p.tot["sOut"])])

    ncw = [0]

    def cdma(out_ap, in_ap, slow=False):
        ncw[0] += 1
        return dma(out_ap, in_ap, "sC", slow=slow)

    cdma(CPD[0:DH, :], projh[:, :])
    cdma(CPD[DH:128, :], projh[:, :])
    cdma(EYE[:, :], eyeh[:, :])
    cdma(SEL8[:, :], sel8h[:, :])
    cdma(B2S[:, :], b2h[:, :])
    cdma(BQ[:, :], bqh.rearrange("o (c p) -> (o p) c", p=128), slow=True)
    cdma(BK[:, :], bkh.rearrange("o (c p) -> (o p) c", p=128), slow=True)
    cdma(BO[:, :], boh.rearrange("o (c p) -> (o p) c", p=128), slow=True)
    cdma(GAM[:, :], gamh.rearrange("o (c p) -> (o p) c", p=128), slow=True)
    cdma(BET[:, :], beth.rearrange("o (c p) -> (o p) c", p=128), slow=True)
    cdma(B1S[:, :, :], b1h.rearrange("e (fc p) -> p e fc", p=128), slow=True)
    cdma(WGS[:, :, :], wgh.rearrange("(c p) e -> p c e", p=128))
    cdma(BGS[:, :], bgh[:, :])
    cdma(BVS[0:1, 0, :], bvh[:, :])
    call = ("sC", 16 * ncw[0])
    t_scale = vop(lambda e: e.tensor_scalar_mul(CPD[:, :], CPD[:, :],
                                                float(C1)), waits=[call])
    vop(lambda e: e.memset(NEG1[:, :], float(NEGH)))
    vop(lambda e: e.memset(ONES[:, :], 1.0))
    twg = vop(lambda e: e.tensor_copy(out=WGHB[:, :, :], in_=WGS[:, :, :]),
              waits=[call])
    twg = vop(lambda e: e.tensor_sub(WGLB[:, :, :], WGS[:, :, :],
                                     WGHB[:, :, :]), waits=[twg], force=True)
    cready = [call, t_scale, twg]

    wst_free = [None, None]
    wst_ready = [None, None]
    wst_i = [0]

    def wst_load(in_ap, ap_dst=None):
        slot = wst_i[0] % 2
        wst_i[0] += 1
        dst = WST[:, slot, :, :] if ap_dst is None else ap_dst(slot)
        tok = dma(dst, in_ap, "sW0" if slot == 0 else "sW1",
                  waits=[wst_free[slot]])
        wst_ready[slot] = tok
        return slot

    mv_free = [None, None]
    mv_ready = [None, None]
    mv_i = [0]

    def mv_load(dst_ap, in_ap, extra=()):
        slot = mv_i[0] % 2
        mv_i[0] += 1
        tok = dma(dst_ap(slot), in_ap, "sM0" if slot == 0 else "sM1",
                  waits=[mv_free[slot]] + list(extra))
        mv_ready[slot] = tok
        return slot

    xt_state = {"free": None}

    def x_load(buf, t0):
        tok = dma(buf[:, :, :],
                  xh.rearrange("(c p) t -> p c t", p=128)[:, :, t0:t0 + TT],
                  "sX", waits=[xt_state["free"]])
        return tok

    bank_free = [None] * 8
    strip_free = [None]

    def _asl(w, dc, j0, j1):
        return w[dc, :, :].rearrange("p (c j) -> p c j", c=NCH)[:, :, j0:j1]
    w1slice = w1h.rearrange("e (c p) f -> p e c f", p=128)
    w2slice = w2h.rearrange("e (f p) d -> p e f d", p=128)

    pp_state = {"i": 0, "free": [None, None], "bfree": [None, None]}
    pp_slices = [("w1", e_i, fcg) for e_i in range(E) for fcg in range(FC)]
    pp_slices += [("w2", e_i, fcg) for e_i in range(E) for fcg in range(FC)]

    def emit_prepass(nslices):
        for _ in range(nslices):
            i = pp_state["i"]
            if i >= len(pp_slices):
                return
            pp_state["i"] = i + 1
            kind, e_i, fcg = pp_slices[i]
            slot = i % 2
            if kind == "w1":
                src = w1slice[:, e_i, :, fcg * 128:(fcg + 1) * 128]
            else:
                src = w2slice[:, e_i, fcg, :].rearrange(
                    "p (a b) -> p a b", a=NCH)
            tin = dma(PPF[slot][:, :, :], src, "sPP",
                      waits=[pp_state["free"][slot]], q="scalar")
            th = vop(lambda e, s_=slot: e.tensor_copy(
                out=PPB[:, s_, 0, :, :], in_=PPF[s_][:, :, :]),
                waits=[tin, pp_state["bfree"][slot]])
            tl = vop(lambda e, s_=slot: e.tensor_sub(
                PPB[:, s_, 1, :, :], PPF[s_][:, :, :], PPB[:, s_, 0, :, :]),
                waits=[th], force=True)
            pp_state["free"][slot] = tl
            if kind == "w1":
                dsth = w1s[e_i, fcg, :, 0:NCH * 128].rearrange(
                    "p (a b) -> p a b", a=NCH)
                dstl = w1s[e_i, fcg, :, NCH * 128:2 * NCH * 128].rearrange(
                    "p (a b) -> p a b", a=NCH)
                dma(dsth, PPB[:, slot, 0, :, :], "sPP", waits=[tl],
                    q="scalar")
                tout = dma(dstl, PPB[:, slot, 1, :, :], "sPP", waits=[tl],
                           q="scalar")
            else:
                for dh2 in range(2):
                    dma(w2s[e_i, fcg, dh2, :, 0, :],
                        PPB[:, slot, 0, :, :].rearrange(
                            "p b c -> p (b c)")[:, dh2 * 512:(dh2 + 1) * 512],
                        "sPP", waits=[tl], q="scalar")
                    tout = dma(w2s[e_i, fcg, dh2, :, 1, :],
                               PPB[:, slot, 1, :, :].rearrange(
                                   "p b c -> p (b c)")[:, dh2 * 512:
                                                       (dh2 + 1) * 512],
                               "sPP", waits=[tl], q="scalar")
            pp_state["bfree"][slot] = tout

    def layernorm(buf, sqs, strip, src_toks, affine, extra_sq_waits=()):
        tsq = aop(lambda e, b_=buf, q_=sqs: e.activation(
            out=q_[:, :, :], in_=b_[:, :, :], func=ACTF.Square, bias=f32(0.0),
            scale=f32(1.0)), waits=list(src_toks) + list(extra_sq_waits))
        t6 = None
        for c in range(NCH):
            t6 = mm(PS[0:1, 6, 0:TT], ONES[:, 0:1], buf[:, c, :], c == 0,
                    c == NCH - 1, waits=[bank_free[6]] + list(src_toks),
                    inc=(c == NCH - 1))
        t7 = None
        for c in range(NCH):
            t7 = mm(PS[0:1, 7, 0:TT], ONES[:, 0:1], sqs[:, c, :], c == 0,
                    c == NCH - 1, waits=[bank_free[7], tsq],
                    inc=(c == NCH - 1))
        tm = vop(lambda e, s_=strip: e.tensor_scalar_mul(
            s_[0:1, 0, :], PS[0:1, 6, 0:TT], float(1.0 / D)),
            waits=[t6, strip_free[0]])
        bank_free[6] = tm
        tmm = vop(lambda e, s_=strip: e.tensor_mul(
            s_[0:1, 1, :], s_[0:1, 0, :], s_[0:1, 0, :]),
            waits=[tm], force=True)
        tv = vop(lambda e, s_=strip: e.scalar_tensor_tensor(
            out=s_[0:1, 1, :], in0=PS[0:1, 7, 0:TT], scalar=float(1.0 / D),
            in1=s_[0:1, 1, :], op0=ALU.mult, op1=ALU.subtract),
            waits=[t7, tmm], force=True)
        bank_free[7] = tv
        ts = aop(lambda e, s_=strip: e.activation(
            out=s_[0:1, 1, :], in_=s_[0:1, 1, :], func=ACTF.Sqrt,
            bias=f32(LN_EPS), scale=f32(1.0)), waits=[tv])
        tr = vop(lambda e, s_=strip: e.reciprocal(s_[0:1, 1, :],
                                                  s_[0:1, 1, :]),
                 waits=[ts])
        tb4 = mm(PS[:, 4, 0:TT], ONES[0:1, 0:128], strip[0:1, 0, :], True,
                 True, waits=[bank_free[4], tm], inc=True)
        tb5 = mm(PS[:, 5, 0:TT], ONES[0:1, 0:128], strip[0:1, 1, :], True,
                 True, waits=[bank_free[5], tr], inc=True)
        strip_free[0] = tb5
        last = None
        for c in range(NCH):
            vop(lambda e, c_=c, b_=buf: e.tensor_sub(
                b_[:, c_, :], b_[:, c_, :], PS[:, 4, 0:TT]), waits=[tb4])
            last = vop(lambda e, c_=c, b_=buf: e.tensor_mul(
                b_[:, c_, :], b_[:, c_, :], PS[:, 5, 0:TT]), waits=[tb5])
            if affine:
                last = vop(lambda e, c_=c, b_=buf: e.tensor_scalar(
                    out=b_[:, c_, :], in0=b_[:, c_, :],
                    scalar1=GAM[:, c_:c_ + 1], scalar2=BET[:, c_:c_ + 1],
                    op0=ALU.mult, op1=ALU.add))
        bank_free[4] = last
        bank_free[5] = last
        return last

    mix_free = None
    ht_free = None
    yb_free = None
    vaug_free = None
    breg_free = []
    for s in range(2):
        set_guard()
        kvz = vop(lambda e: e.memset(KV[:, :, :, :], 0.0),
                  waits=(cready if s == 0 else ()))
        vones = vop(lambda e: e.memset(P1_VAUG[:, :, :, DH:DH + 1], 1.0))
        for ti in range(NTS):
            t0 = s * SEQ + ti * TT
            if s == 0:
                emit_prepass(32)
            xw = x_load(P1_XT, t0)
            kv_last_mm = None
            for half in range(2):
                vlast = None
                for oc in range(4):
                    oct_ = half * 4 + oc
                    ms = mv_load(lambda sl: P1_WMV[:, sl, :, :],
                                 _asl(wv, oct_, 0, 128),
                                 extra=breg_free if oc == 0 else ())
                    for tt in range(NSUB):
                        b = (oc * NSUB + tt) % 2
                        mm(PS[:, b, 0:128], ONES[0:1, 0:128],
                           BVS[0:1, 0, oct_ * 128:(oct_ + 1) * 128], True,
                           False,
                           waits=[bank_free[b], mv_ready[ms], xw] + cready)
                        tok = None
                        for c in range(NCH):
                            tok = mm(PS[:, b, 0:128],
                                     P1_XT[:, c, tt * 128:(tt + 1) * 128],
                                     P1_WMV[:, ms, c, :], False, c == NCH - 1,
                                     inc=(c == NCH - 1))
                        if tt == NSUB - 1:
                            mv_free[ms] = tok
                        vlast = vop(lambda e, b_=b, tt_=tt, q_=oc:
                                    e.tensor_copy(
                                        out=P1_VAUG[:, tt_,
                                                    q_ * 2:q_ * 2 + 2, 0:DH],
                                        in_=PS[:, b_, 0:128].rearrange(
                                            "p (a b) -> p a b", a=2)),
                                    waits=[tok, vaug_free])
                        bank_free[b] = vlast
                vdone = vlast
                evtoks = []
                for cl in range(4):
                    cglob = half * 4 + cl
                    ws = wst_load(_asl(wk, cglob, 0, 128))
                    b = 2 + (cl % 2)
                    tok = None
                    for c in range(NCH):
                        tok = mm(PS[:, b, 0:TT], WST[:, ws, c, :],
                                 P1_XT[:, c, :], c == 0, c == NCH - 1,
                                 waits=([bank_free[b], wst_ready[ws], xw,
                                         mv_free[0], mv_free[1]]
                                        if c == 0 else ()),
                                 inc=(c == NCH - 1))
                    wst_free[ws] = tok
                    ev = aop(lambda e, b_=b, cl_=cl, cg=cglob:
                             e.activation(out=P1_KQ[:, cl_, :],
                                          in_=PS[:, b_, 0:TT],
                                          func=ACTF.Identity,
                                          bias=BK[:, cg:cg + 1],
                                          scale=f32(1.0)),
                             waits=[tok])
                    evtoks.append(ev)
                    bank_free[b] = ev
                sq2_free = None
                tsq = None
                for cl in range(4):
                    cglob = half * 4 + cl
                    tsq = aop(lambda e, cl_=cl:
                              e.activation(out=P1_SQ2[:, :],
                                           in_=P1_KQ[:, cl_, :],
                                           func=ACTF.Square, bias=f32(0.0),
                                           scale=f32(1.0)),
                              waits=[evtoks[cl], sq2_free])
                    for dlt in range(2):
                        h = 2 * cglob + dlt
                        hloc = h - 8 * half
                        base = dlt * 64
                        eps_toks = []
                        for tt in range(NSUB):
                            b = tt % 2
                            mm(PS[:, b, 0:NF],
                               P1_KQ[base:base + DH, cl,
                                     tt * 128:(tt + 1) * 128],
                               CPD[base:base + DH, :], True, False,
                               waits=[bank_free[b], evtoks[cl]])
                            tok = mm(PS[:, b, 0:NF],
                                     P1_SQ2[base:base + DH,
                                            tt * 128:(tt + 1) * 128],
                                     NEG1[base:base + DH, 0:NF], False, True,
                                     waits=[tsq], inc=True)
                            sq2_free = tok
                            te = aop(lambda e, b_=b, tt_=tt:
                                     e.activation(out=P1_KP[:, tt_, :],
                                                  in_=PS[:, b_, 0:NF],
                                                  func=ACTF.Exp,
                                                  bias=f32(LNRATIO),
                                                  scale=f32(1.0)),
                                     waits=[tok])
                            bank_free[b] = te
                            tp = vop(lambda e, tt_=tt:
                                     e.tensor_scalar_add(P1_KP[:, tt_, :],
                                                         P1_KP[:, tt_, :],
                                                         float(EPSR)),
                                     waits=[te])
                            eps_toks.append(tp)
                        for fc2 in range(2):
                            b = 4 + fc2
                            tok = None
                            for tt in range(NSUB):
                                tok = mm(PS[:, b, 0:DH + 1],
                                         P1_KP[:, tt,
                                               fc2 * 128:(fc2 + 1) * 128],
                                         P1_VAUG[:, tt, hloc, :], tt == 0,
                                         tt == NSUB - 1,
                                         waits=([bank_free[b], eps_toks[tt],
                                                 vdone, kvz, vones]
                                                if tt == 0 else
                                                [eps_toks[tt]]),
                                         inc=(tt == NSUB - 1))
                            kv_last_mm = tok
                            tkv = vop(lambda e, b_=b, f_=fc2, h_=h:
                                      e.tensor_add(KV[:, f_, h_, :],
                                                   KV[:, f_, h_, :],
                                                   PS[:, b_, 0:DH + 1]),
                                      waits=[tok])
                            bank_free[b] = tkv
                vaug_free = kv_last_mm
                breg_free = [kv_last_mm, tsq]
            xt_state["free"] = kv_last_mm
        kv_done_dv = ("sDV", p.tot["sDV"])
        set_guard()
        for ti in range(NTS):
            t0 = s * SEQ + ti * TT
            if s == 0:
                emit_prepass(32)
            xw = x_load(P2_XT, t0)
            zin_last = None
            sq2_free = None
            kq_free = None
            for half in range(2):
                evtoks = []
                for cl in range(4):
                    cglob = half * 4 + cl
                    ws = wst_load(_asl(wq, cglob, 0, 128))
                    b = 2 + (cl % 2)
                    tok = None
                    for c in range(NCH):
                        tok = mm(PS[:, b, 0:TT], WST[:, ws, c, :],
                                 P2_XT[:, c, :], c == 0, c == NCH - 1,
                                 waits=([bank_free[b], wst_ready[ws], xw]
                                        if c == 0 else ()),
                                 inc=(c == NCH - 1))
                    wst_free[ws] = tok
                    ev = aop(lambda e, b_=b, cl_=cl, cg=cglob:
                             e.activation(out=P2_KQ[:, cl_, :],
                                          in_=PS[:, b_, 0:TT],
                                          func=ACTF.Identity,
                                          bias=BQ[:, cg:cg + 1],
                                          scale=f32(1.0)),
                             waits=[tok, kq_free])
                    evtoks.append(ev)
                    bank_free[b] = ev
                for cl in range(4):
                    cglob = half * 4 + cl
                    tsq = aop(lambda e, cl_=cl:
                              e.activation(out=P2_SQ2[:, :],
                                           in_=P2_KQ[:, cl_, :],
                                           func=ACTF.Square, bias=f32(0.0),
                                           scale=f32(1.0)),
                              waits=[evtoks[cl], sq2_free])
                    for dlt in range(2):
                        h = 2 * cglob + dlt
                        base = dlt * 64
                        eps_toks = []
                        for fc2 in range(2):
                            b = fc2
                            mm(PS[:, b, 0:TT],
                               CPD[base:base + DH, fc2 * 128:(fc2 + 1) * 128],
                               P2_KQ[base:base + DH, cl, :], True, False,
                               waits=[bank_free[b], evtoks[cl]])
                            tok = mm(PS[:, b, 0:TT],
                                     NEG1[base:base + DH, 0:128],
                                     P2_SQ2[base:base + DH, :], False, True,
                                     waits=[tsq], inc=True)
                            sq2_free = tok
                            te = aop(lambda e, b_=b, f_=fc2:
                                     e.activation(out=P2_QP[:, f_, :],
                                                  in_=PS[:, b_, 0:TT],
                                                  func=ACTF.Exp,
                                                  bias=f32(LNRATIO),
                                                  scale=f32(1.0)),
                                     waits=[tok])
                            bank_free[b] = te
                            tp = vop(lambda e, f_=fc2:
                                     e.tensor_scalar_add(P2_QP[:, f_, :],
                                                         P2_QP[:, f_, :],
                                                         float(EPSR)),
                                     waits=[te])
                            eps_toks.append(tp)
                        ba = 4 + (h % 2)
                        bn = 6 + (h % 2)
                        tok = None
                        for fc2 in range(2):
                            tok = mm(PS[base:base + DH, ba, 0:TT],
                                     KV[:, fc2, h, 0:DH],
                                     P2_QP[:, fc2, :], fc2 == 0, fc2 == 1,
                                     waits=([bank_free[ba], eps_toks[0],
                                             eps_toks[1], kv_done_dv]
                                            if fc2 == 0 else ()),
                                     inc=(fc2 == 1))
                        qkv_mm = tok
                        tok = None
                        for fc2 in range(2):
                            tok = mm(PS[0:1, bn, 0:TT],
                                     KV[:, fc2, h, DH:DH + 1],
                                     P2_QP[:, fc2, :], fc2 == 0, fc2 == 1,
                                     waits=([bank_free[bn]] if fc2 == 0
                                            else ()),
                                     inc=(fc2 == 1))
                        nrm_mm = tok
                        tn = vop(lambda e, b_=bn:
                                 e.tensor_scalar_add(P2_STRIP[0:1, 0, :],
                                                     PS[0:1, b_, 0:TT],
                                                     float(1e-6)),
                                 waits=[nrm_mm, strip_free[0]])
                        tn = vop(lambda e: e.reciprocal(P2_STRIP[0:1, 1, :],
                                                        P2_STRIP[0:1, 0, :]),
                                 waits=[tn], force=True)
                        tb = mm(PS[:, bn, 0:TT], ONES[0:1, 0:128],
                                P2_STRIP[0:1, 1, :], True, True,
                                waits=[tn], inc=True)
                        strip_free[0] = tb
                        vop(lambda e, b_=ba, bs=base, c2=cglob:
                            e.tensor_copy(out=P2_ZIN[bs:bs + DH, c2, :],
                                          in_=PS[bs:bs + DH, b_, 0:TT]),
                            waits=[qkv_mm])
                        zin_last = vop(
                            lambda e, bn_=bn, bs=base, c2=cglob:
                            e.tensor_mul(P2_ZIN[bs:bs + DH, c2, :],
                                         P2_ZIN[bs:bs + DH, c2, :],
                                         PS[bs:bs + DH, bn_, 0:TT]),
                            waits=[tb])
                        bank_free[ba] = zin_last
                        bank_free[bn] = zin_last
                    kq_free = ("sPE", p.tot["sPE"])
            ev_toks = []
            wo_mm_last = None
            for dc in range(NCH):
                ws = wst_load(_asl(wo, dc, 0, 128))
                b = 2 + (dc % 2)
                tok = None
                for c in range(NCH):
                    tok = mm(PS[:, b, 0:TT], WST[:, ws, c, :],
                             P2_ZIN[:, c, :], c == 0, c == NCH - 1,
                             waits=([bank_free[b], wst_ready[ws], zin_last]
                                    if c == 0 else ()),
                             inc=(c == NCH - 1))
                wst_free[ws] = tok
                wo_mm_last = tok
                ev = vop(lambda e, b_=b, dc_=dc:
                         e.scalar_tensor_tensor(
                             out=P2_XT[:, dc_, :], in0=PS[:, b_, 0:TT],
                             scalar=BO[:, dc_:dc_ + 1],
                             in1=P2_XT[:, dc_, :],
                             op0=ALU.add, op1=ALU.add),
                         waits=[tok])
                ev_toks.append(ev)
                bank_free[b] = ev
            layernorm(P2_XT, P2_SQS, P2_STRIP, ev_toks, affine=False,
                      extra_sq_waits=[wo_mm_last])
            cplast = None
            t0l = ti * TT
            for c in range(NCH):
                cplast = vop(lambda e, c_=c, t0_=t0l: e.tensor_copy(
                    out=REG[:, c_, t0_:t0_ + TT], in_=P2_XT[:, c_, :]))
            xt_state["free"] = cplast
            if debug:
                dma(dbg["y1"].rearrange("(c p) t -> p c t", p=128)
                    [:, :, t0:t0 + TT], REG[:, :, t0l:t0l + TT], "sOut",
                    waits=[cplast])
        ppdone = ("sPP", p.tot["sPP"])
        for ti in range(NTS):
            set_guard()
            if ti == 0:
                guard.append(ppdone)
            t0l = ti * TT
            t0 = s * SEQ + t0l
            regslot = REG[:, :, t0l:t0l + TT]
            ylast = None
            for c in range(NCH):
                th = vop(lambda e, c_=c, r_=regslot: e.tensor_copy(
                    out=YHB[:, c_, :], in_=r_[:, c_, :]), waits=[yb_free])
                ylast = vop(lambda e, c_=c, r_=regslot: e.tensor_sub(
                    YLB[:, c_, :], r_[:, c_, :], YHB[:, c_, :]),
                    waits=[th], force=True)
            wtoks = []
            for tt in range(NSUB):
                b = 6 + (tt % 2)
                mm(PS[:, b, 0:E], ONES[0:1, 0:128], BGS[0:1, :], True, False,
                   waits=[bank_free[b], ylast])
                tok = None
                for c in range(NCH):
                    sl = slice(tt * 128, (tt + 1) * 128)
                    mm(PS[:, b, 0:E], YHB[:, c, sl], WGHB[:, c, :], False, False)
                    mm(PS[:, b, 0:E], YHB[:, c, sl], WGLB[:, c, :], False, False)
                    tok = mm(PS[:, b, 0:E], YLB[:, c, sl], WGHB[:, c, :], False,
                             c == NCH - 1, inc=(c == NCH - 1))
                t1 = vop(lambda e, b_=b: e.reduce_max(GS[:, 24:25],
                                                      PS[:, b_, 0:E], axis=AX.X),
                         waits=[tok])
                teq = vop(lambda e, b_=b: e.tensor_scalar(
                    out=GS[:, 0:8], in0=PS[:, b_, 0:E], scalar1=GS[:, 24:25],
                    scalar2=None, op0=ALU.is_equal), waits=[t1], force=True)
                tst = vop(lambda e, b_=b: e.scalar_tensor_tensor(
                    out=GS[:, 0:8], in0=GS[:, 0:8], scalar=float(-1e30),
                    in1=PS[:, b_, 0:E], op0=ALU.mult, op1=ALU.add),
                    waits=[teq], force=True)
                tm2 = vop(lambda e: e.reduce_max(GS[:, 26:27], GS[:, 0:8],
                                                 axis=AX.X), waits=[tst],
                          force=True)
                tneg = vop(lambda e: e.tensor_scalar_mul(GS[:, 25:26],
                                                         GS[:, 24:25],
                                                         float(-1.0)))
                tgem = vop(lambda e, b_=b: e.tensor_scalar(
                    out=GS[:, 8:16], in0=PS[:, b_, 0:E], scalar1=GS[:, 26:27],
                    scalar2=None, op0=ALU.is_ge), waits=[tm2], force=True)
                tel = aop(lambda e, b_=b: e.activation(
                    out=GS[:, 16:24], in_=PS[:, b_, 0:E], func=ACTF.Exp,
                    bias=GS[:, 25:26], scale=f32(1.0)), waits=[tneg, tgem])
                tdn = aop(lambda e: e.activation(
                    out=GS[:, 27:28], in_=GS[:, 26:27], func=ACTF.Exp,
                    bias=GS[:, 25:26], scale=f32(1.0)))
                bank_free[b] = [tgem, tel]
                tda = vop(lambda e: e.tensor_scalar_add(GS[:, 27:28],
                                                        GS[:, 27:28],
                                                        float(1.0)), waits=[tdn])
                trdn = vop(lambda e: e.reciprocal(GS[:, 28:29], GS[:, 27:28]),
                           waits=[tda], force=True)
                vop(lambda e, tt_=tt: e.tensor_mul(WTM[:, tt_, :], GS[:, 16:24],
                                                   GS[:, 8:16]),
                    waits=[tel, tgem], force=True)
                tw = vop(lambda e, tt_=tt: e.tensor_scalar_mul(
                    WTM[:, tt_, :], WTM[:, tt_, :], GS[:, 28:29]),
                    waits=[trdn], force=True)
                wtoks.append(tw)
            trtoks = []
            for tt in range(NSUB):
                tok = p.op("tensor",
                           lambda e, tt_=tt: e.transpose(
                               PS[0:E, 7, tt_ * 128:(tt_ + 1) * 128],
                               WTM[:, tt_, :], EYE[:, :]),
                           waits=[wtoks[tt], bank_free[7]] + guard,
                           incs=(("sPE", 1),))
                trtoks.append(tok)
            tw8 = vop(lambda e: e.tensor_copy(out=WFM8[:, :],
                                              in_=PS[0:E, 7, 0:TT]),
                      waits=trtoks)
            bank_free[7] = tw8
            eo_ev_last = None
            for e_i in range(E):
                twb = mm(PS[:, 6, 0:TT], SEL8[:, e_i * 128:(e_i + 1) * 128],
                         WFM8[:, :], True, True,
                         waits=[tw8, bank_free[6]], inc=True)
                twbc = vop(lambda e: e.tensor_copy(out=F_WBC[:, :],
                                                   in_=PS[:, 6, 0:TT]),
                           waits=[twb])
                bank_free[6] = twbc
                lim_toks = []
                for fcg in range(FC):
                    ws = wst_load(w1s[e_i, fcg, :, :].rearrange('p (l c j) -> p l c j', l=2, c=NCH),
                                  ap_dst=lambda sl: WSB[:, sl, :, :, :])
                    b = fcg % 2
                    tok = None
                    for c in range(NCH):
                        mm(PS[:, b, 0:TT], WSB[:, ws, 0, c, :], YHB[:, c, :],
                           c == 0, False,
                           waits=([bank_free[b], wst_ready[ws], ylast]
                                  if c == 0 else ()))
                        mm(PS[:, b, 0:TT], WSB[:, ws, 0, c, :], YLB[:, c, :],
                           False, False)
                        tok = mm(PS[:, b, 0:TT], WSB[:, ws, 1, c, :],
                                 YHB[:, c, :], False, c == NCH - 1,
                                 inc=(c == NCH - 1))
                    wst_free[ws] = tok
                    trl = aop(lambda e, b_=b, fc_=fcg, e__=e_i:
                              e.activation(out=F_RELU[:, :], in_=PS[:, b_, 0:TT],
                                           func=ACTF.Relu,
                                           bias=B1S[:, e__, fc_:fc_ + 1],
                                           scale=f32(1.0)),
                              waits=[tok, ht_free,
                                     lim_toks[-1] if lim_toks else None])
                    bank_free[b] = trl
                    tmu = vop(lambda e: e.tensor_mul(
                        F_RELU[:, :], F_RELU[:, :], F_WBC[:, :]),
                        waits=[trl, twbc])
                    thh = vop(lambda e, fc_=fcg: e.tensor_copy(
                        out=HTH[:, fc_, :], in_=F_RELU[:, :]),
                        waits=[tmu], force=True)
                    tll = vop(lambda e, fc_=fcg: e.tensor_sub(
                        HTL[:, fc_, :], F_RELU[:, :], HTH[:, fc_, :]),
                        waits=[thh], force=True)
                    lim_toks.append(tll)
                for dh2 in range(2):
                    gtoks = [None] * 4
                    if e_i == 0:
                        for dl in range(4):
                            dc = dh2 * 4 + dl
                            gtoks[dl] = mm(PS[:, 2 + dl, 0:TT],
                                           B2S[:, dc * 128:(dc + 1) * 128],
                                           WFM8[:, :], True, False,
                                           waits=[bank_free[2 + dl], tw8])
                    for fcg in range(FC):
                        wm = mv_load(lambda sl: W2B[:, sl, :, :],
                                     w2s[e_i, fcg, dh2, :, :, :])
                        for dl in range(4):
                            st = (fcg == 0) and (e_i != 0)
                            sp = (fcg == FC - 1)
                            w = [mv_ready[wm], lim_toks[fcg]]
                            if fcg == 0 and e_i != 0:
                                w.append(bank_free[2 + dl])
                            sl = slice(dl * 128, (dl + 1) * 128)
                            mm(PS[:, 2 + dl, 0:TT], W2B[:, wm, 0, sl],
                               HTH[:, fcg, :], st, False, waits=w)
                            mm(PS[:, 2 + dl, 0:TT], W2B[:, wm, 0, sl],
                               HTL[:, fcg, :], False, False)
                            gtoks[dl] = mm(PS[:, 2 + dl, 0:TT],
                                           W2B[:, wm, 1, sl],
                                           HTH[:, fcg, :], False, sp,
                                           inc=(sp or dl == 3))
                        mv_free[wm] = gtoks[3]
                    for dl in range(4):
                        dc = dh2 * 4 + dl
                        if e_i == 0:
                            ev = vop(lambda e, dl_=dl, dc_=dc: e.tensor_copy(
                                out=F_MIX[:, dc_, :], in_=PS[:, 2 + dl_, 0:TT]),
                                waits=[gtoks[dl], mix_free])
                        else:
                            ev = vop(lambda e, dl_=dl, dc_=dc: e.tensor_add(
                                F_MIX[:, dc_, :], F_MIX[:, dc_, :],
                                PS[:, 2 + dl_, 0:TT]),
                                waits=[gtoks[dl]])
                        bank_free[2 + dl] = ev
                        eo_ev_last = ev
                ht_free = ("sPE", p.tot["sPE"])
            yb_free = ("sPE", p.tot["sPE"])
            add_toks = []
            for c in range(NCH):
                add_toks.append(vop(lambda e, c_=c, r_=regslot: e.tensor_add(
                    F_MIX[:, c_, :], F_MIX[:, c_, :], r_[:, c_, :]),
                    waits=[eo_ev_last]))
            layernorm(F_MIX, F_SQS, F_STRIP, [add_toks[-1]], affine=True,
                      extra_sq_waits=[ht_free])
            add_toks = []
            for c in range(NCH):
                add_toks.append(vop(lambda e, c_=c, r_=regslot: e.tensor_add(
                    F_MIX[:, c_, :], F_MIX[:, c_, :], r_[:, c_, :])))
            lnlast = layernorm(F_MIX, F_SQS, F_STRIP, [add_toks[-1]],
                               affine=False)
            if debug:
                dma(dbg["mix"].rearrange("(c p) t -> p c t", p=128)
                    [:, :, t0:t0 + TT], F_MIX[:, :, :], "sOut", waits=[lnlast])
            dma(outh.rearrange("(c p) t -> p c t", p=128)[:, :, t0:t0 + TT],
                F_MIX[:, :, :], "sOut", waits=[lnlast])
            mix_free = ("sOut", p.tot["sOut"])



    p.op("sync", None, waits=[("sOut", p.tot["sOut"])])

    import contextlib
    es = contextlib.ExitStack()
    sems = {s: es.enter_context(nc.semaphore(s)) for s in p.tot}
    block = es.enter_context(nc.Block())

    def emit(eng_name):
        def body(eng):
            for fn, waits, incs in p.items[eng_name]:
                for s, v in waits:
                    eng.wait_ge(sems[s], v)
                if fn is None:
                    continue
                instr = fn(eng)
                for s, d in incs:
                    instr.then_inc(sems[s], d)
        return body

    block.sync(emit("sync"))
    block.tensor(emit("tensor"))
    block.vector(emit("vector"))
    block.scalar(emit("scalar"))
    block.gpsimd(emit("gpsimd"))
    es.close()
    return nc


def _wre(w):
    w = np.asarray(w, np.float32).reshape(NCH, 128, NCH, 128)
    return np.ascontiguousarray(w.transpose(2, 1, 0, 3).reshape(NCH, 128, D))


def make_inmaps(inputs):
    q = np.asarray(inputs["L0_attn_queries"], np.float32)
    base = {
        "wq": _wre(inputs["L0_attn_Wq"]),
        "wk": _wre(inputs["L0_attn_Wk"]),
        "wv": _wre(inputs["L0_attn_Wv"]),
        "wo": _wre(inputs["L0_attn_Wo"]),
        "bq": np.asarray(inputs["L0_attn_bq"], np.float32).reshape(1, D),
        "bk": np.asarray(inputs["L0_attn_bk"], np.float32).reshape(1, D),
        "bv": np.asarray(inputs["L0_attn_bv"], np.float32).reshape(1, D),
        "bo": np.asarray(inputs["L0_attn_bo"], np.float32).reshape(1, D),
        "proj": np.ascontiguousarray(inputs["L0_attn_proj"], np.float32),
        "wg": np.ascontiguousarray(inputs["L0_ffn_Wg"], np.float32),
        "bg": np.asarray(inputs["L0_ffn_bg"], np.float32).reshape(1, E),
        "w1": np.ascontiguousarray(inputs["L0_ffn_W1"], np.float32),
        "b1": np.ascontiguousarray(inputs["L0_ffn_b1"], np.float32),
        "w2": np.ascontiguousarray(inputs["L0_ffn_W2"], np.float32),
        "b2": np.ascontiguousarray(inputs["L0_ffn_b2"], np.float32),
        "gam": np.asarray(inputs["L0_ffn_gamma"], np.float32).reshape(1, D),
        "bet": np.asarray(inputs["L0_ffn_beta"], np.float32).reshape(1, D),
        "eye": np.eye(128, dtype=np.float32),
        "sel8": np.kron(np.eye(E, dtype=np.float32),
                        np.ones((1, 128), np.float32)),
    }
    in_maps = []
    for c in range(8):
        m = dict(base)
        m["x"] = np.ascontiguousarray(q[2 * c:2 * c + 2].reshape(T, D).T)
        in_maps.append(m)
    return in_maps


def kernel(**inputs):
    in_maps = make_inmaps(inputs)
    nc = build_program()
    res = run_bass_kernel_spmd(nc, in_maps, core_ids=list(range(8)))
    outs = [np.asarray(r["out"]) for r in res.results]
    full = np.stack([o.T.reshape(2, SEQ, D) for o in outs])
    return np.ascontiguousarray(full.reshape(16, SEQ, D), dtype=np.float32)
